# version 04
# baseline (speedup 1.0000x reference)
import numpy as np
import concourse.bass as bass
import concourse.mybir as mybir
from concourse.bass_utils import run_bass_kernel_spmd

F32 = mybir.dt.float32
BF16 = mybir.dt.bfloat16
I32 = mybir.dt.int32
AF = mybir.ActivationFunctionType
OP = mybir.AluOpType
AX = mybir.AxisListType
ET = mybir.EngineType

L = 4096
D = 1024
TO = 2048
DIH = 1024
NST = 16
NH = 8
DH = 64
NE = 8
HF = 2048
TC = 512
T2 = 256
SC = 1024
NSC = L // SC
NEG = -1.0e30
EPS = 1e-5

ENGINES = ("sync", "tensor", "vector", "scalar", "gpsimd")
CONSTVALS = [EPS, 1.0, -0.5, -1.0, 0.125, 0.0, 2.0, -2.0]


class Sem:
    def __init__(self, nc, name):
        self.h = nc.alloc_semaphore(name)
        self.n = 0


class Prog:
    def __init__(self, nc):
        self.nc = nc
        self.steps = {e: [] for e in ENGINES}

    def emit(self, eng, fn=None, waits=(), inc=None, inck=1):
        w = [(s.h, int(v)) for (s, v) in waits]
        ih = None
        if inc is not None:
            inc.n += inck
            ih = inc.h
        self.steps[eng].append((fn, w, ih, inck))

    def dma(self, eng, sem, out, in_, waits=()):
        self.emit(eng, lambda e, o=out, i=in_: e.dma_start(out=o, in_=i),
                  waits=waits, inc=sem, inck=16)

    def finalize(self):
        with self.nc.Block() as blk:
            def run(eng_name):
                def body(eng):
                    for fn, waits, ih, ik in self.steps[eng_name]:
                        for (h, v) in waits:
                            eng.wait_ge(h, v)
                        instr = fn(eng) if fn is not None else None
                        if ih is not None:
                            if instr is None:
                                instr = eng.nop()
                            instr.then_inc(ih, ik)
                return body
            blk.sync(run("sync"))
            blk.tensor(run("tensor"))
            blk.vector(run("vector"))
            blk.scalar(run("scalar"))
            blk.gpsimd(run("gpsimd"))


def build(nc, dbg=None, stage=99, ffn_lim=(4, 8)):
    dbg = dict(dbg or {})
    p = Prog(nc)

    def din(name, shape, dt=F32):
        return nc.dram_tensor(name, list(shape), dt, kind="ExternalInput")

    hid = din("hid", [L, D]).ap()
    hidown = din("hidown", [TO, D]).ap()
    gofs_d = din("gofs", [1, 4], I32).ap()
    inxw_d = din("inxw", [D, DIH]).ap()
    inzw_d = din("inzw", [D, DIH]).ap()
    convw_d = din("convw", [128, 8, 4]).ap()
    convb_d = din("convb", [128, 8]).ap()
    xpw_d = din("xpw", [DIH, 96]).ap()
    dtw_d = din("dtw", [64, DIH]).ap()
    dtb_d = din("dtb", [128, 8]).ap()
    alog_d = din("alog", [128, 8 * NST]).ap()
    dvec_d = din("dvec", [128, 8]).ap()
    opw_d = din("opw", [2 * DIH, D]).ap()
    wqk_d = din("wqk", [D, 2, NH * DH]).ap()
    wv_d = din("wv", [D, NH * DH]).ap()
    bqk_d = din("bqk", [1, 2, NH * DH]).ap()
    bv_d = din("bv", [1, NH * DH]).ap()
    wo_d = din("wo", [128, 8, 8, 128]).ap()
    bo_d = din("bo", [1, D]).ap()
    gw_d = din("gw", [D, NE]).ap()
    gb_d = din("gb", [NE, 1]).ap()
    w1_d = din("w1", [NE, D, HF]).ap()
    b1_d = din("b1", [128, NE, HF // 128]).ap()
    w2_d = din("w2", [NE, HF, D]).ap()
    w1b_d = nc.dram_tensor("w1b", [NE, D, HF], BF16).ap()
    w2b_d = nc.dram_tensor("w2b", [NE, HF, D], BF16).ap()
    b2_d = din("b2", [NE, D]).ap()

    out_d = nc.dram_tensor("out", [TO, D], F32, kind="ExternalOutput").ap()
    dbg_d = {}
    for name, shape in dbg.items():
        dbg_d[name] = nc.dram_tensor("dbg_" + name, list(shape), F32,
                                     kind="ExternalOutput").ap()

    xp_in = [nc.dram_tensor(f"xp_in{s}", [96, SC], F32) for s in range(NSC)]
    xp_out = [nc.dram_tensor(f"xp_out{s}", [96, SC], F32) for s in range(NSC)]
    y_in = [nc.dram_tensor(f"y_in{s}", [DIH, SC], BF16) for s in range(NSC)]
    y_out = [nc.dram_tensor(f"y_out{s}", [2 * DIH, SC], BF16) for s in range(NSC)]
    x3_in = nc.dram_tensor("x3_in", [D, TO], BF16)
    x3_out = nc.dram_tensor("x3_out", [2 * D, TO], BF16)
    at_in = nc.dram_tensor("at_in", [NH * DH, L], BF16)
    at_out = nc.dram_tensor("at_out", [2 * NH * DH, L], BF16)
    xn2_hbm = nc.dram_tensor("xn2_hbm", [D, TO], BF16)
    GROUPS = [[0, 1], [2, 3], [4, 5], [6, 7]]

    def A(name, shape, dt):
        return nc.alloc_sbuf_tensor(name, list(shape), dt)

    r_sb = A("r_sb", [128, 8, TO], F32)
    states = A("states", [128, 128], F32)
    ident_bf = A("ident_bf", [128, 128], BF16)
    ident_f = A("ident_f", [128, 128], F32)
    trimask_bf = A("trimask_bf", [128, 128], BF16)
    negt_bf = A("negt_bf", [128, 128], BF16)
    ones_row = A("ones_row", [1, TC], BF16)
    ones_col = A("ones_col", [1, 128], BF16)
    ones128 = A("ones128", [128, 1], BF16)
    rowsel = A("rowsel", [128, DH], BF16)
    esel = A("esel", [NE, NE, 128], BF16)
    a_sb = A("a_sb", [128, 8 * NST], F32)
    alog_sb = A("alog_sb", [128, 8 * NST], F32)
    convw_sb = A("convw_sb", [128, 8, 4], F32)
    convb_sb = A("convb_sb", [128, 8], F32)
    dtb_sb = A("dtb_sb", [128, 8], F32)
    dvec_sb = A("dvec_sb", [128, 8], F32)
    gb_sb = A("gb_sb", [NE, 1], F32)
    b1_sb = A("b1_sb", [128, NE, HF // 128], F32)
    b2_sb = A("b2_sb", [NE, NE, 128], BF16)
    gw_sb = A("gw_sb", [128, 8, NE], BF16)
    gofs_sb = A("gofs_sb", [1, 4], I32)
    cvals = A("cvals", [128, 8], F32)

    AF32 = A("AF32", [128, 10240], F32)
    AB16 = A("AB16", [128, 46592], BF16)
    af = AF32.ap()
    ab = AB16.ap()

    def _view(apx, shape):
        if len(shape) == 1:
            return apx
        names = [f"a{i}" for i in range(len(shape))]
        kw = {nm: s for nm, s in zip(names[1:], shape[1:])}
        return apx.rearrange("p (" + " ".join(names) + ") -> p " + " ".join(names), **kw)

    def vf(off, *shape):
        n = int(np.prod(shape))
        assert off + n <= 10240, ("vf", off, shape)
        return _view(af[:, off:off + n], shape)

    def vb(off, *shape):
        n = int(np.prod(shape))
        assert off + n <= 46592, ("vb", off, shape)
        return _view(ab[:, off:off + n], shape)

    inxw = vb(0, 8, 8, 128)
    inzw = vb(8192, 8, 8, 128)
    opw = vb(0, 16, 8, 128)
    xpw = vb(16384, 8, 96)
    dtw = vb(17152, 8, 128)
    u_bf = vb(18176, 8, SC)
    yfull = vb(18176, 16, TC)
    sz_bf = vb(26368, 8, SC)
    xp_dtb = vb(44832, SC)
    xp_bcb = vb(34560, 4, T2)
    xin = vb(35584, 8, 4 + TC)
    xn1 = vb(39712, 8, TC)
    m2_hC = vb(39712, 2, T2)
    m2_y2 = vb(40224, 8, T2)
    xn1_tm = vb(43808, D)
    f_w1e = vb(0, 2, 8, 128)
    f_w2c = vb(16384, 2, 16, 128)
    f_hp = vb(20480, 16, TC)
    f_x2c = vb(28672, 2, 8, TC)
    f_wr = vb(36864, TC)
    f_wfb = vb(37376, 4, TC)
    stb = vb(39424, 1024)
    f_lbf = vb(20480, 8, TC)
    f_lsq = vb(24576, 8, TC)
    f_rel = vb(40448, 2, TC)
    xn3st = vb(16384, 8, TC)
    at_vtok = vb(0, 32, NH, 72)
    at_q = vb(18432, 8, TC)
    at_k = vb(22528, 8, TC)
    at_p = vb(26624, 2, 2, TC)
    at_x3 = vb(28672, 8, TC)
    at_onr = vb(32768, 2, TC)
    bo_sb = vb(33792, D)
    bqk_sb = vb(34816, 2, NH * DH)
    bv_sb = vb(35840, NH * DH)
    at_wqk = vb(36352, 8, 2, 128)
    at_wv = vb(38400, 8, TC)
    at_sg = vb(38400, 2, 8, TC)
    at_denb = vb(28672, 8, TC)
    at_wo = vb(18432, 8, 8, 128)
    at_at = vb(26624, 8, TC)

    xp_part = vf(0, SC)
    m1_xall = vf(2048, D)
    m1_xsq = vf(3072, D)
    m1_cva = vf(4096, TC)
    m1_cvb = vf(4608, TC)
    m1_st_t = A("m1_st_t", [128, 16], F32)
    m1_st = m1_st_t.ap()
    delta = vf(2048, 8, T2)
    sp_v = vf(4096, T2)
    sp_a = vf(4352, T2)
    sp_b = vf(4608, T2)
    dA_b = vf(4864, 2, T2)
    dBu_b = vf(5376, 2, T2)
    h_b = vf(5888, 2, T2)
    du_all = vf(6400, 8, T2)
    brepB = vf(8448, T2)
    brepC = vf(8704, T2)
    t1_sb = vf(8960, T2)
    ln_t = vf(9216, TC)
    ln_t2 = vf(9728, TC)
    g_lgT = vf(0, 4, NE)
    g_ex = vf(32, 4, NE)
    g_l2 = vf(64, 4, NE)
    g_wk = vf(96, 4, NE)
    g_s = vf(128, 4)
    g_m1 = vf(132, 4)
    g_m2 = vf(136, 4)
    g_rc = vf(140, 4)
    g_wfm = vf(256, 4, TC)
    ln_tF = vf(2304, TC)
    ln_t2F = vf(2560, TC)
    stf = vf(2816, 1536)
    at_den = vf(0, 8, TC)

    ps2a = nc.alloc_psum_tensor("ps2a", [128, 1024], F32)
    ps2b = nc.alloc_psum_tensor("ps2b", [128, 1024], F32)
    ps_c = nc.alloc_psum_tensor("ps_c", [128, 512], F32)
    ps_d = nc.alloc_psum_tensor("ps_d", [128, 512], F32)
    ps_e = nc.alloc_psum_tensor("ps_e", [128, 512], F32)
    ps_h = nc.alloc_psum_tensor("ps_h", [128, 512], F32)
    psT = ps_h.ap().bitcast(BF16)
    psm = [ps2a.ap()[:, 0:512], ps2a.ap()[:, 512:1024],
           ps2b.ap()[:, 0:512], ps2b.ap()[:, 512:1024],
           ps_c.ap()[:, :], ps_d.ap()[:, :], ps_e.ap()[:, :], ps_h.ap()[:, :]]

    sDW = Sem(nc, "sDW")
    sDL = Sem(nc, "sDL")
    sDS = Sem(nc, "sDS")
    sT = Sem(nc, "sT")
    sV = Sem(nc, "sV")
    sS = Sem(nc, "sS")
    sG = Sem(nc, "sG")
    sC = Sem(nc, "sC")
    sO = Sem(nc, "sO")
    sY = Sem(nc, "sY")
    sAR = Sem(nc, "sAR")
    sLN = Sem(nc, "sLN")
    sAI = Sem(nc, "sAI")
    sW1p = [Sem(nc, "sW1a"), Sem(nc, "sW1b")]
    sW2 = [Sem(nc, "sW2a"), Sem(nc, "sW2b")]
    sWC = Sem(nc, "sWC")

    def tap(name, src_ap, waits):
        if name in dbg_d:
            p.dma("gpsimd", sDW, dbg_d[name][:, :], src_ap, waits=waits)
            for eng in ENGINES:
                p.emit(eng, None, waits=[(sDW, sDW.n)])

    wm = {}
    for nm, (dst, src) in {
        "inxw": (inxw[:, :, :, :], inxw_d.rearrange("(kc k) (mc m) -> k kc mc m", k=128, m=128)),
        "inzw": (inzw[:, :, :, :], inzw_d.rearrange("(kc k) (mc m) -> k kc mc m", k=128, m=128)),
        "xpw": (xpw[:, :, :], xpw_d.rearrange("(kc k) m -> k kc m", k=128)),
        "dtw": (dtw[0:64, :, :], dtw_d.rearrange("k (mc m) -> k mc m", m=128)),
        "b2": (b2_sb[:, :, :], b2_d.rearrange("e (mc m) -> e mc m", m=128)),
        "gw": (gw_sb[:, :, :], gw_d.rearrange("(kc k) m -> k kc m", k=128)),
    }.items():
        p.dma("gpsimd", sDW, dst, src)
    for nm in ("inxw", "inzw", "xpw", "dtw", "b2", "gw"):
        wm[nm] = sDW.n
    p.dma("sync", sDL, gofs_sb[:, :], gofs_d[:, :])
    for nm, (dst, src) in {
        "convw": (convw_sb[:, :, :], convw_d[:, :, :]),
        "convb": (convb_sb[:, :], convb_d[:, :]),
        "dtb": (dtb_sb[:, :], dtb_d[:, :]),
        "dvec": (dvec_sb[:, :], dvec_d[:, :]),
        "gb": (gb_sb[:, :], gb_d[:, :]),
        "b1": (b1_sb[:, :, :], b1_d[:, :, :]),
        "alog": (alog_sb[:, :], alog_d[:, :]),
    }.items():
        p.dma("sync", sDL, dst, src)
    for nm in ("convw", "convb", "dtb", "dvec", "gb", "b1", "alog"):
        wm[nm] = sDL.n
    for e_ in range(NE):
        p.dma("gpsimd", sWC, w1b_d[e_], w1_d[e_])
        p.dma("gpsimd", sWC, w2b_d[e_], w2_d[e_])
    mcast = sWC.n
    m0 = sDL.n
    regs4 = [nc.alloc_register(ET.SP, f"gofs{j}") for j in range(4)]
    for j in range(4):
        p.emit("sync", lambda e, j=j: e.reg_load(regs4[j], gofs_sb[0:1, j:j + 1]),
               waits=([(sDL, m0)] if j == 0 else []))

    def _cpair(mfn, afn):
        p.emit("gpsimd", mfn, inc=sG)
        mk = sG.n
        p.emit("gpsimd", afn, waits=[(sG, mk)], inc=sG)

    _cpair(lambda e: e.memset(ident_bf[:], 1.0),
           lambda e: e.affine_select(ident_bf[:], ident_bf[:], [[1, 128]],
                                     OP.is_equal, 0.0, channel_multiplier=-1))
    _cpair(lambda e: e.memset(ident_f[:], 1.0),
           lambda e: e.affine_select(ident_f[:], ident_f[:], [[1, 128]],
                                     OP.is_equal, 0.0, channel_multiplier=-1))
    _cpair(lambda e: e.memset(trimask_bf[:], 0.0),
           lambda e: e.affine_select(trimask_bf[:], trimask_bf[:], [[1, 128]],
                                     OP.is_ge, NEG, channel_multiplier=-1))
    _cpair(lambda e: e.memset(rowsel[:], 1.0),
           lambda e: e.affine_select(rowsel[:], rowsel[:], [[0, DH]],
                                     OP.is_equal, 0.0, base=-64,
                                     channel_multiplier=1))
    _cpair(lambda e: e.memset(esel[:], 1.0),
           lambda e: e.affine_select(esel[:], esel[:], [[1, NE], [0, 128]],
                                     OP.is_equal, 0.0, channel_multiplier=-1))

    for dst_, val_ in [(negt_bf, NEG), (ones_row, 1.0), (ones_col, 1.0),
                       (ones128, 1.0)]:
        p.emit("gpsimd", lambda e, d=dst_, v=val_: e.memset(d[:], v), inc=sG)
    for col, val in enumerate(CONSTVALS):
        p.emit("gpsimd", lambda e, col=col, val=val: e.memset(
            cvals[:, col:col + 1], val), inc=sG)
    gC = sG.n
    p.emit("vector", lambda e: e.memset(xin[:, :, 0:4], 0.0), inc=sV)
    for col, val in enumerate(CONSTVALS):
        nc.const_aps.register(F32, val, cvals.ap()[:, col:col + 1])
    p.emit("scalar", None, waits=[(sG, gC)])
    p.emit("vector", None, waits=[(sG, gC)])
    p.emit("vector", lambda e: e.memset(states[:], 0.0), inc=sV)
    p.emit("scalar", lambda e: e.activation(out=a_sb[:], in_=alog_sb[:], func=AF.Exp),
           waits=[(sDL, wm["alog"])], inc=sS)
    mA = sS.n
    p.emit("vector", lambda e: e.tensor_scalar_mul(a_sb[:], a_sb[:], -1.0),
           waits=[(sS, mA)], inc=sV)
    mAv = sV.n

    for tt in range(16):
        p.dma("sync", sDL, m1_xall[:, :], hidown[tt * 128:(tt + 1) * 128, :],
              waits=([(sDL, m0), (sT, sT.n)] if tt == 0 else [(sT, sT.n)]))
        mload = sDL.n
        for half in range(2):
            psd = ps_c if half == 0 else ps_d
            for j in range(4):
                kc = half * 4 + j
                p.emit("tensor", lambda e, kc=kc, j=j, psd=psd: e.transpose(
                    psd.ap()[:, j * 128:(j + 1) * 128],
                    m1_xall[:, kc * 128:(kc + 1) * 128], ident_f.ap()[:, :]),
                    waits=([(sDL, mload), (sG, gC), (sV, sV.n)] if j == 0 else []),
                    inc=(sT if j == 3 else None))
            mt = sT.n
            p.emit("vector", lambda e, half=half, tt=tt, psd=psd: e.tensor_copy(
                r_sb.ap()[:, half * 4:half * 4 + 4, tt * 128:(tt + 1) * 128],
                psd.ap()[:, 0:512].rearrange("p (a b) -> p a b", b=128)),
                waits=[(sT, mt)], inc=sV)

    scan_marks = []

    def m1_chunk(tc):
        t0 = tc * TC
        scq = tc % (SC // TC)
        ucols = slice(scq * TC, (scq + 1) * TC)
        for tt in range(4):
            p.dma("sync", sDL, m1_xall[:, :], hid[t0 + tt * 128:t0 + (tt + 1) * 128, :],
                  waits=[(sS, sS.n), (sT, sT.n), (sV, sV.n)])
            mload = sDL.n
            p.emit("vector", lambda e: e.reduce_sum(m1_st[:, 0:1], m1_xall[:, :], axis=AX.X),
                   waits=[(sDL, mload)], inc=sV)
            p.emit("scalar", lambda e: e.activation(out=m1_xsq[:, :], in_=m1_xall[:, :],
                                                    func=AF.Square),
                   waits=[(sDL, mload), (sV, sV.n)], inc=sS)
            msq = sS.n
            p.emit("vector", lambda e: e.reduce_sum(m1_st[:, 1:2], m1_xsq[:, :], axis=AX.X),
                   waits=[(sS, msq)], inc=sV)
            p.emit("vector", lambda e: e.tensor_scalar_mul(m1_st[:, 2:3], m1_st[:, 0:1], 1.0 / D),
                   waits=[(sV, sV.n)], inc=sV)
            p.emit("vector", lambda e: e.tensor_scalar_mul(m1_st[:, 3:4], m1_st[:, 1:2], 1.0 / D),
                   inc=sV)
            p.emit("vector", lambda e: e.tensor_mul(m1_st[:, 4:5], m1_st[:, 2:3], m1_st[:, 2:3]),
                   waits=[(sV, sV.n)], inc=sV)
            p.emit("vector", lambda e: e.tensor_sub(m1_st[:, 5:6], m1_st[:, 3:4], m1_st[:, 4:5]),
                   waits=[(sV, sV.n)], inc=sV)
            mv = sV.n
            p.emit("scalar", lambda e: e.activation(out=m1_st[:, 6:7], in_=m1_st[:, 5:6],
                                                    func=AF.Ln, bias=EPS),
                   waits=[(sV, mv)], inc=sS)
            p.emit("scalar", lambda e: e.activation(out=m1_st[:, 7:8], in_=m1_st[:, 6:7],
                                                    func=AF.Exp, scale=-0.5),
                   waits=[(sS, sS.n)], inc=sS)
            ms2 = sS.n
            p.emit("vector", lambda e: e.scalar_tensor_tensor(
                m1_st[:, 8:9], m1_st[:, 2:3], -1.0, m1_st[:, 7:8], OP.mult, OP.mult),
                waits=[(sS, ms2)], inc=sV)
            mv2 = sV.n
            p.emit("vector", lambda e: e.tensor_scalar(
                xn1_tm[:, :], m1_xall[:, :], m1_st[:, 7:8], m1_st[:, 8:9],
                OP.mult, OP.add), waits=[(sV, mv2)], inc=sV)
            ms3 = sV.n
            if tc == 0 and tt == 0:
                tap("xall0", m1_xall[:, :], [(sV, ms3)])
                tap("m1st0", m1_st[:, :], [(sV, ms3)])
                tap("xn1tm0", None, []) if False else None
                tap("xntm0", xn1_tm[:, :], [(sV, ms3)])
            for grp in range(2):
                for j in range(4):
                    jj = grp * 4 + j
                    p.emit("tensor", lambda e, jj=jj, j=j: e.transpose(
                        psT[:, j * 128:(j + 1) * 128],
                        xn1_tm[:, jj * 128:(jj + 1) * 128], ident_bf.ap()[:, :]),
                        waits=([(sV, sV.n)] if j == 0 else []),
                        inc=(sT if j == 3 else None))
                mtt = sT.n
                p.emit("vector", lambda e, grp=grp, tt=tt: e.tensor_copy(
                    xn1[:, grp * 4:grp * 4 + 4, tt * 128:(tt + 1) * 128],
                    psT[:, 0:512].rearrange("p (a b) -> p a b", b=128)),
                    waits=([(sT, mtt), (sY, sY.n)] if (tt == 0 and grp == 0)
                           else [(sT, mtt)]), inc=sV)
        mxn1 = sV.n
        if tc == 0:
            tap("xn10", xn1.rearrange("p a b -> p (a b)"), [(sV, mxn1)])
        for part in range(2):
            ww = inxw if part == 0 else inzw
            for mc in range(8):
                psd = ps_c if (mc % 2 == 0) else ps_d
                w0 = [(sV, mxn1), (sDW, wm["inzw"])] if (part == 0 and mc == 0) else []
                w0 = w0 + [(sS, sS.n)]
                for kc in range(8):
                    p.emit("tensor", lambda e, ww=ww, mc=mc, kc=kc, psd=psd: e.matmul(
                        psd.ap()[:, :], ww[:, kc, mc, :], xn1[:, kc, :],
                        start=(kc == 0), stop=(kc == 7)),
                        waits=(w0 if kc == 0 else []),
                        inc=(sT if kc == 7 else None))
                mt = sT.n
                if part == 0:
                    p.emit("scalar", lambda e, mc=mc, psd=psd: e.activation(
                        out=xin[:, mc, 4:4 + TC], in_=psd.ap()[:, :], func=AF.Copy),
                        waits=[(sT, mt), (sV, sV.n)], inc=sS)
                else:
                    p.emit("scalar", lambda e, mc=mc, psd=psd, ucols=ucols: e.activation(
                        out=sz_bf[:, mc, ucols], in_=psd.ap()[:, :], func=AF.Silu),
                        waits=[(sT, mt)], inc=sS)
        for mc in range(8):
            ms = sS.n
            p.emit("vector", lambda e, mc=mc: e.tensor_scalar(
                m1_cva[:, :], xin[:, mc, 1:1 + TC], convw_sb.ap()[:, mc, 0:1],
                convb_sb.ap()[:, mc:mc + 1], OP.mult, OP.add),
                waits=[(sS, ms), (sDL, wm["convb"])], inc=sV)
            p.emit("vector", lambda e, mc=mc: e.scalar_tensor_tensor(
                m1_cvb[:, :], xin[:, mc, 2:2 + TC], convw_sb.ap()[:, mc, 1:2],
                m1_cva[:, :], OP.mult, OP.add), waits=[(sV, sV.n)], inc=sV)
            p.emit("vector", lambda e, mc=mc: e.scalar_tensor_tensor(
                m1_cva[:, :], xin[:, mc, 3:3 + TC], convw_sb.ap()[:, mc, 2:3],
                m1_cvb[:, :], OP.mult, OP.add), waits=[(sV, sV.n)], inc=sV)
            p.emit("vector", lambda e, mc=mc: e.scalar_tensor_tensor(
                m1_cvb[:, :], xin[:, mc, 4:4 + TC], convw_sb.ap()[:, mc, 3:4],
                m1_cva[:, :], OP.mult, OP.add), waits=[(sV, sV.n)], inc=sV)
            mcv = sV.n
            p.emit("scalar", lambda e, mc=mc, ucols=ucols: e.activation(
                out=u_bf[:, mc, ucols], in_=m1_cvb[:, :], func=AF.Silu),
                waits=[(sV, mcv)], inc=sS)
        msu = sS.n
        p.emit("vector", lambda e: e.tensor_copy(xin[:, :, 1:4], xin[:, :, 513:516]),
               waits=[(sV, sV.n), (sS, msu)], inc=sV)
        for kc in range(8):
            p.emit("tensor", lambda e, kc=kc, ucols=ucols: e.matmul(
                ps_e.ap()[0:96, :], xpw[:, kc, :], u_bf[:, kc, ucols],
                start=(kc == 0), stop=(kc == 7)),
                waits=([(sS, msu), (sV, sV.n)] if kc == 0 else []),
                inc=(sT if kc == 7 else None))
        mt = sT.n
        p.emit("vector", lambda e, ucols=ucols: e.tensor_copy(
            xp_part[0:96, ucols], ps_e.ap()[0:96, :]),
            waits=[(sT, mt), (sAR, sAR.n)], inc=sV)

    def m2_chunk(sc, q, mxpl):
        cols = slice(q * T2, (q + 1) * T2)
        p.dma("gpsimd", sDW, xp_bcb[0:8, :, :],
              xp_out[sc].ap()[64:96, q * T2:(q + 1) * T2].rearrange(
                  "(blk pp) t -> pp blk t", pp=8),
              waits=([(sT, sT.n), (sDW, mxpl)] if q == 0 else [(sT, sT.n)]))
        mbcl = sDW.n
        for mc in range(8):
            psd = ps_c if mc % 2 == 0 else ps_d
            p.emit("tensor", lambda e, mc=mc, psd=psd, cols=cols: e.matmul(
                psd.ap()[:, 0:T2], dtw[0:64, mc, :], xp_dtb[0:64, cols],
                start=True, stop=True),
                waits=[(sV, sV.n), (sS, sS.n), (sDW, sDW.n)], inc=sT)
            mt = sT.n
            p.emit("vector", lambda e, psd=psd, mc=mc: e.tensor_scalar(
                sp_v[:, :], psd.ap()[:, 0:T2], dtb_sb.ap()[:, mc:mc + 1], None,
                OP.add, OP.bypass),
                waits=[(sT, mt)], inc=sV)
            mv = sV.n
            p.emit("scalar", lambda e: e.activation(out=sp_a[:, :], in_=sp_v[:, :],
                                                    func=AF.Abs), waits=[(sV, mv)], inc=sS)
            p.emit("scalar", lambda e: e.activation(out=sp_b[:, :], in_=sp_a[:, :],
                                                    func=AF.Exp, scale=-1.0),
                   waits=[(sS, sS.n)], inc=sS)
            p.emit("scalar", lambda e: e.activation(out=sp_a[:, :], in_=sp_b[:, :],
                                                    func=AF.Ln, bias=1.0),
                   waits=[(sS, sS.n)], inc=sS)
            ms = sS.n
            p.emit("vector", lambda e, mc=mc: e.scalar_tensor_tensor(
                delta[:, mc, :], sp_v[:, :], 0.0, sp_a[:, :], OP.max, OP.add),
                waits=[(sS, ms)], inc=sV)
        for dt in range(8):
            p.emit("vector", lambda e, dt=dt, cols=cols: e.tensor_mul(
                du_all[:, dt, :], delta[:, dt, :], u_bf[:, dt, cols]),
                waits=([(sV, sV.n)] if dt == 0 else []), inc=sV)
        mdu = sV.n
        if sc == 0 and q == 0:
            tap("asb", a_sb.ap()[:, :], [(sV, mdu)])
            tap("delta0", delta.rearrange("p a b -> p (a b)"), [(sV, mdu), (sS, sS.n)])
            tap("du0", du_all.rearrange("p a b -> p (a b)"), [(sV, mdu)])
        for n in range(NST):
            jB, jC = n, 16 + n
            p.emit("tensor", lambda e, jB=jB: e.matmul(
                ps_e.ap()[:, 0:T2], esel.ap()[0:8, jB % 8, :],
                xp_bcb[0:8, jB // 8, :], start=True, stop=False),
                waits=[(sV, mdu), (sS, sS.n), (sDW, mbcl)], inc=sT)
            p.emit("tensor", lambda e, jC=jC: e.matmul(
                ps_e.ap()[:, T2:2 * T2], esel.ap()[0:8, jC % 8, :],
                xp_bcb[0:8, jC // 8, :], start=False, stop=True), inc=sT)
            mt = sT.n
            p.emit("scalar", lambda e: e.activation(out=brepB[:, :], in_=ps_e.ap()[:, 0:T2],
                                                    func=AF.Copy), waits=[(sT, mt)], inc=sS)
            p.emit("scalar", lambda e: e.activation(out=brepC[:, :], in_=ps_e.ap()[:, T2:2 * T2],
                                                    func=AF.Copy), inc=sS)
            mbc = sS.n
            for dt in range(8):
                it = len(scan_marks)
                bi = it % 2
                colidx = dt * 16 + n
                wback = [(sV, scan_marks[-2][0])] if len(scan_marks) >= 2 else []
                whc = [scan_marks[-2][1]] if len(scan_marks) >= 2 else []
                p.emit("scalar", lambda e, dt=dt, bi=bi, colidx=colidx: e.activation(
                    out=dA_b[:, bi, :], in_=delta[:, dt, :], func=AF.Exp,
                    scale=a_sb.ap()[:, colidx:colidx + 1]),
                    waits=wback + ([(sV, mAv), (sV, mdu)] if (n == 0 and dt == 0) else []),
                    inc=sS)
                mda = sS.n
                gw0 = wback + ([(sS, mbc)] if dt == 0 else [])
                if n == 0 and dt == 0:
                    gw0 = gw0 + [(sV, mdu)]
                p.emit("gpsimd", lambda e, dt=dt, bi=bi: e.tensor_tensor(
                    dBu_b[:, bi, :], du_all[:, dt, :], brepB[:, :], OP.mult),
                    waits=gw0, inc=sG)
                mbu = sG.n
                p.emit("vector", lambda e, bi=bi, colidx=colidx: e.tensor_tensor_scan(
                    h_b[:, bi, :], dA_b[:, bi, :], dBu_b[:, bi, :],
                    states.ap()[:, colidx:colidx + 1], OP.mult, OP.add),
                    waits=[(sS, mda), (sG, mbu)] + whc, inc=sV)
                msc = sV.n
                p.emit("scalar", lambda e, bi=bi, colidx=colidx: e.activation(
                    out=states.ap()[:, colidx:colidx + 1], in_=h_b[:, bi, T2 - 1:T2],
                    func=AF.Copy), waits=[(sV, msc)], inc=sS)
                wyb = [(sT, scan_marks[-2][2])] if len(scan_marks) >= 2 else []
                if it % 2 == 0:
                    p.emit("vector", lambda e, bi=bi: e.tensor_mul(
                        m2_hC[:, bi, :], h_b[:, bi, :], brepC[:, :]),
                        waits=wyb + [(sV, msc)], inc=sV)
                    mhc = (sV, sV.n)
                else:
                    p.emit("gpsimd", lambda e, bi=bi: e.tensor_mul(
                        m2_hC[:, bi, :], h_b[:, bi, :], brepC[:, :]),
                        waits=wyb + [(sV, msc)], inc=sG)
                    mhc = (sG, sG.n)
                ps_y = ps2a if dt < 4 else ps2b
                off = (dt % 4) * T2
                p.emit("tensor", lambda e, ps_y=ps_y, off=off, bi=bi, n=n, dt=dt: e.matmul(
                    ps_y.ap()[:, off:off + T2], ident_bf.ap()[:, :], m2_hC[:, bi, :],
                    start=(n == 0 and dt % 2 == 0),
                    stop=(n == NST - 1 and dt % 2 == 1)),
                    waits=[mhc], inc=sT)
                scan_marks.append((msc, mhc, sT.n))
                if sc == 0 and q == 0 and it == 0:
                    tap("dA00", dA_b[:, 0, :], [(sS, mda)])
                    tap("dBu00", dBu_b[:, 0, :], [(sG, mbu)])
                    tap("h00", h_b[:, 0, :], [(sV, msc)])
                    tap("hC00", m2_hC[:, 0, :], [mhc])
        myd = sT.n
        for dt in range(8):
            ps_y = ps2a if dt < 4 else ps2b
            off = (dt % 4) * T2
            p.emit("vector", lambda e, dt=dt, ps_y=ps_y, off=off, cols=cols:
                   e.scalar_tensor_tensor(
                       t1_sb[:, :], u_bf[:, dt, cols], dvec_sb.ap()[:, dt:dt + 1],
                       ps_y.ap()[:, off:off + T2], OP.mult, OP.add),
                   waits=([(sT, myd), (sDL, wm["dvec"]), (sY, sY.n)] if dt == 0
                          else [(sV, sV.n)]), inc=sV)
            p.emit("vector", lambda e, dt=dt, cols=cols: e.tensor_mul(
                m2_y2[:, dt, :], t1_sb[:, :], sz_bf[:, dt, cols]),
                waits=[(sV, sV.n)], inc=sV)
        my2 = sV.n
        if sc == 0 and q == 0:
            tap("y20", m2_y2.rearrange("p a b -> p (a b)"), [(sV, my2)])
        p.dma("scalar", sY,
              y_in[sc].ap().rearrange("(dt pp) t -> pp dt t", pp=128)[:, :, cols],
              m2_y2[:, :, :],
              waits=([(sV, my2), (sC, sC.n)] if q == 0 else [(sV, my2)]))

    def m3_chunk(sc, mop):
        yo = y_out[sc]
        ap_in = bass.AP(yo, regs4[0], [[SC, 128], [SC * 128, 16], [1, TC]])
        p.dma("sync", sDL, yfull[:, :, :], ap_in, waits=[(sC, sC.n), (sT, sT.n)])
        mld = sDL.n
        rcols = slice(sc * TC, (sc + 1) * TC)
        for mc in range(8):
            psd = ps_c if mc % 2 == 0 else ps_d
            for kc in range(16):
                p.emit("tensor", lambda e, mc=mc, kc=kc, psd=psd: e.matmul(
                    psd.ap()[:, :], opw[:, kc, mc, :], yfull[:, kc, :],
                    start=(kc == 0), stop=(kc == 15)),
                    waits=([(sDL, mld), (sDW, mop), (sV, sV.n), (sS, sS.n)] if kc == 0 else []),
                    inc=(sT if kc == 15 else None))
            mt = sT.n
            p.emit("vector", lambda e, mc=mc, psd=psd, rcols=rcols: e.tensor_add(
                ln_t[:, :], r_sb.ap()[:, mc, rcols], psd.ap()[:, :]),
                waits=[(sT, mt)], inc=sV)
            mv = sV.n
            p.emit("scalar", lambda e, mc=mc, rcols=rcols: e.activation(
                out=r_sb.ap()[:, mc, rcols], in_=ln_t[:, :], func=AF.Copy),
                waits=[(sV, mv)], inc=sS)

    for sc in range(NSC if stage >= 3 else (1 if stage >= 2 else 0)):
        for q in range(SC // TC):
            m1_chunk(sc * (SC // TC) + q)
        if stage < 3:
            break
        mxp = sV.n
        p.dma("scalar", sAR, xp_in[sc].ap()[:, :], xp_part[0:96, :],
              waits=[(sV, mxp), (sC, sC.n)])
        mds = sAR.n
        p.emit("gpsimd", lambda e, sc=sc: e.collective_compute(
            "AllReduce", OP.add, replica_groups=GROUPS,
            ins=[xp_in[sc].ap().opt()], outs=[xp_out[sc].ap().opt()]),
            waits=[(sAR, mds)], inc=sC)
        mcc = sC.n
        p.dma("gpsimd", sDW, xp_dtb[0:64, :], xp_out[sc].ap()[0:64, :],
              waits=[(sC, mcc), (sT, sT.n)])
        mxpl = sDW.n
        p.emit("vector", None, waits=[(sDW, mxpl)], inc=sV)
        if sc == 0:
            tap("u0", u_bf.rearrange("p a b -> p (a b)"), [(sV, sV.n), (sS, sS.n)])
            tap("sz0", sz_bf.rearrange("p a b -> p (a b)"), [(sV, sV.n), (sS, sS.n)])
            tap("xpd0", xp_dtb[0:64, :], [(sDW, mxpl)])
            
        if stage < 4:
            continue
        for q in range(SC // T2):
            m2_chunk(sc, q, mxpl)
        mys = sY.n
        p.emit("gpsimd", lambda e, sc=sc: e.collective_compute(
            "AllGather", OP.bypass, replica_groups=GROUPS,
            ins=[y_in[sc].ap().opt()], outs=[y_out[sc].ap().opt()]),
            waits=[(sY, mys)], inc=sC)
    if stage >= 5:
        p.dma("gpsimd", sDW, opw[:, :, :, :],
              opw_d.rearrange("(kc k) (mc m) -> k kc mc m", k=128, m=128),
              waits=[(sT, sT.n)])
        mop = sDW.n
        for sc in range(NSC):
            m3_chunk(sc, mop)
        tap("r_mamba", r_sb.ap().rearrange("p a b -> p (a b)"), [(sS, sS.n)])

    def ln_chunk(tcl, dst, lt, lt2):
        rc = slice(tcl * TC, (tcl + 1) * TC)
        for mc in range(8):
            p.emit("vector", lambda e, mc=mc, rc=rc: e.tensor_copy(
                f_lbf[:, mc, :], r_sb.ap()[:, mc, rc]),
                waits=([(sS, sS.n), (sT, sT.n), (sLN, sLN.n)] if mc == 0 else []),
                inc=sV)
            p.emit("scalar", lambda e, mc=mc, rc=rc: e.activation(
                out=f_lsq[:, mc, :], in_=r_sb.ap()[:, mc, rc], func=AF.Square),
                waits=([(sV, sV.n)] if mc == 0 else []), inc=sS)
        mkv, mks = sV.n, sS.n
        for mc in range(8):
            p.emit("tensor", lambda e, mc=mc: e.matmul(
                ps_e.ap()[0:1, :], ones128.ap()[:, :], f_lbf[:, mc, :],
                start=(mc == 0), stop=(mc == 7)),
                waits=([(sV, mkv)] if mc == 0 else []), inc=(sT if mc == 7 else None))
        for mc in range(8):
            p.emit("tensor", lambda e, mc=mc: e.matmul(
                ps_c.ap()[0:1, :], ones128.ap()[:, :], f_lsq[:, mc, :],
                start=(mc == 0), stop=(mc == 7)),
                waits=([(sS, mks)] if mc == 0 else []), inc=(sT if mc == 7 else None))
        mkt = sT.n
        p.emit("vector", lambda e: e.tensor_scalar_mul(stf[0:1, 0:512],
                                                       ps_e.ap()[0:1, :], 1.0 / D),
               waits=[(sT, mkt)], inc=sV)
        p.emit("vector", lambda e: e.tensor_scalar_mul(stf[0:1, 512:1024],
                                                       ps_c.ap()[0:1, :], 1.0 / D), inc=sV)
        p.emit("vector", lambda e: e.tensor_mul(stf[0:1, 1024:1536],
                                                stf[0:1, 0:512],
                                                stf[0:1, 0:512]),
               waits=[(sV, sV.n)], inc=sV)
        p.emit("vector", lambda e: e.tensor_sub(stf[0:1, 512:1024],
                                                stf[0:1, 512:1024],
                                                stf[0:1, 1024:1536]),
               waits=[(sV, sV.n)], inc=sV)
        mv2 = sV.n
        p.emit("scalar", lambda e: e.activation(out=stf[0:1, 1024:1536],
                                                in_=stf[0:1, 512:1024],
                                                func=AF.Ln, bias=EPS),
               waits=[(sV, mv2)], inc=sS)
        p.emit("scalar", lambda e: e.activation(out=stf[0:1, 512:1024],
                                                in_=stf[0:1, 1024:1536],
                                                func=AF.Exp, scale=-0.5),
               waits=[(sS, sS.n)], inc=sS)
        ms = sS.n
        p.emit("vector", lambda e: e.scalar_tensor_tensor(
            stf[0:1, 0:512], stf[0:1, 0:512], -1.0, stf[0:1, 512:1024],
            OP.mult, OP.mult), waits=[(sS, ms)], inc=sV)
        p.emit("vector", lambda e: e.tensor_copy(stb[0:1, :], stf[0:1, 0:1024]),
               waits=[(sV, sV.n)], inc=sV)
        mv4 = sV.n
        p.emit("tensor", lambda e: e.matmul(ps_d.ap()[:, :], ones_col.ap()[0:1, :],
                                            stb[0:1, 512:1024], start=True, stop=True),
               waits=[(sV, mv4)], inc=sT)
        p.emit("tensor", lambda e: e.matmul(ps_e.ap()[:, :], ones_col.ap()[0:1, :],
                                            stb[0:1, 0:512], start=True, stop=True),
               inc=sT)
        mkt2 = sT.n
        for mc in range(8):
            p.emit("vector", lambda e, mc=mc, rc=rc, lt2=lt2: e.tensor_mul(
                lt2[:, :], r_sb.ap()[:, mc, rc], ps_d.ap()[:, :]),
                waits=([(sT, mkt2)] if mc == 0 else [(sV, sV.n)]), inc=sV)
            p.emit("vector", lambda e, mc=mc, dst=dst, lt2=lt2: e.tensor_add(
                dst[:, mc, :], lt2[:, :], ps_e.ap()[:, :]),
                waits=[(sV, sV.n)], inc=sV)
        return sV.n

    ffn_calls = [0]
    f_acc = vf(5376, 8, TC)

    def emit_ffn_final():
        emit_gating()
        ffn_calls[0] += 1
        first_call = ffn_calls[0] == 1
        if first_call:
            tap("gwf0", g_wfm[0:8, :, :].rearrange("p a b -> p (a b)"),
                [(sV, gate_mark[0])])
        w2free = [sT.n, sT.n]
        w1free = [sT.n, sT.n]
        mevac = [0, 0]
        for tcl in range(ffn_lim[0]):
            bi = tcl % 2
            p.dma("sync", sDL, f_x2c[:, bi, :, :],
                  xn2_hbm.ap().rearrange("(kc k) t -> k kc t", k=128)[
                      :, :, tcl * TC:(tcl + 1) * TC],
                  waits=[(sT, sT.n), (sLN, sLN.n)])
            mld = sDL.n
            for e_i in range(ffn_lim[1]):
                w1dmk = [0] * 16
                for hc in range(2):
                    p.dma("gpsimd", sW1p[hc], f_w1e[:, hc, :, :],
                          w1b_d[e_i, :, hc * 128:(hc + 1) * 128].rearrange(
                              "(kc k) m -> k kc m", k=128),
                          waits=([(sT, w1free[hc]), (sWC, mcast)]
                                 if (tcl == 0 and e_i == 0) else [(sT, w1free[hc])]))
                    w1dmk[hc] = sW1p[hc].n
                w2dmk = [0] * 8
                for mc in range(2):
                    p.dma("gpsimd", sW2[mc], f_w2c[:, mc, :, :],
                          w2b_d[e_i, :, mc * 128:(mc + 1) * 128].rearrange(
                              "(hc k) m -> k hc m", k=128),
                          waits=([(sT, w2free[mc]), (sWC, mcast)]
                                 if (tcl == 0 and e_i == 0) else [(sT, w2free[mc])]))
                    w2dmk[mc] = sW2[mc].n
                p.emit("tensor", lambda e, e_i=e_i, tcl=tcl: e.matmul(
                    ps_e.ap()[:, :], esel.ap()[0:8, e_i, :], f_wfb[0:8, tcl, :],
                    start=True, stop=True),
                    waits=[(sV, sV.n), (sS, sS.n)], inc=sT)
                mre = sT.n
                p.emit("scalar", lambda e: e.activation(out=f_wr[:, :],
                                                        in_=ps_e.ap()[:, :], func=AF.Copy),
                       waits=[(sT, mre)], inc=sS)
                mwr = sS.n
                relmark = {}
                vmark = {}
                for hc in range(16):
                    psd = ps_c if hc % 2 == 0 else ps_d
                    if hc == 0:
                        w0 = [(sDL, mld), (sS, mwr)]
                    elif hc == 1:
                        w0 = [(sS, mwr)]
                    else:
                        w0 = [(sS, relmark[hc - 2])]
                    w0 = w0 + [(sW1p[hc % 2], w1dmk[hc])]
                    for kc in range(8):
                        p.emit("tensor", lambda e, hc=hc, kc=kc, psd=psd, bi=bi: e.matmul(
                            psd.ap()[:, :], f_w1e[:, hc % 2, kc, :], f_x2c[:, bi, kc, :],
                            start=(kc == 0), stop=(kc == 7)),
                            waits=(w0 if kc == 0 else []),
                            inc=(sT if kc == 7 else None))
                    mt = sT.n
                    w1free[hc % 2] = mt
                    if hc + 2 < 16:
                        nhc = hc + 2
                        p.dma("gpsimd", sW1p[nhc % 2], f_w1e[:, nhc % 2, :, :],
                              w1b_d[e_i, :, nhc * 128:(nhc + 1) * 128].rearrange(
                                  "(kc k) m -> k kc m", k=128),
                              waits=[(sT, w1free[nhc % 2])])
                        w1dmk[nhc] = sW1p[nhc % 2].n
                    p.emit("scalar", lambda e, psd=psd, e_i=e_i, hc=hc: e.activation(
                        out=f_rel[:, hc % 2, :], in_=psd.ap()[:, :], func=AF.Relu,
                        bias=b1_sb.ap()[:, e_i, hc:hc + 1]),
                        waits=[(sT, mt)]
                              + ([(sV, vmark[hc - 2])] if hc >= 2 else [(sV, sV.n)])
                              + ([(sDL, wm["b1"])] if hc == 0 else []),
                        inc=sS)
                    relmark[hc] = sS.n
                    p.emit("vector", lambda e, hc=hc: e.tensor_mul(
                        f_hp[:, hc, :], f_rel[:, hc % 2, :], f_wr[:, :]),
                        waits=[(sS, relmark[hc])], inc=sV)
                    vmark[hc] = sV.n
                mhp = sV.n
                if first_call and tcl == 0 and e_i == 0:
                    tap("fwr0", f_wr[:, :], [(sS, mwr)])
                    tap("fhp0", f_hp.rearrange("p a b -> p (a b)"), [(sV, mhp)])
                for mc in range(8):
                    bank = ps_e if mc % 2 == 0 else ps_h
                    for hc in range(16):
                        def mmfn(e, mc=mc, hc=hc, e_i=e_i, tcl=tcl, bank=bank):
                            if hc == 0 and e_i == 0:
                                e.matmul(bank.ap()[:, :], b2_sb.ap()[0:8, mc, :],
                                         f_wfb[0:8, tcl, :], start=True, stop=False)
                                return e.matmul(
                                    bank.ap()[:, :], f_w2c[:, mc % 2, hc, :],
                                    f_hp[:, hc, :], start=False, stop=False)
                            return e.matmul(
                                bank.ap()[:, :], f_w2c[:, mc % 2, hc, :],
                                f_hp[:, hc, :], start=(hc == 0), stop=(hc == 15))
                        p.emit("tensor", mmfn,
                               waits=([(sV, mhp), (sW2[mc % 2], w2dmk[mc]),
                                       (sV, mevac[mc % 2]), (sS, mwr)]
                                      if hc == 0 else []),
                               inc=(sT if hc == 15 else None))
                    w2free[mc % 2] = sT.n
                    mt2 = sT.n
                    if mc + 2 < 8:
                        nmc = mc + 2
                        p.dma("gpsimd", sW2[nmc % 2], f_w2c[:, nmc % 2, :, :],
                              w2b_d[e_i, :, nmc * 128:(nmc + 1) * 128].rearrange(
                                  "(hc k) m -> k hc m", k=128),
                              waits=[(sT, w2free[nmc % 2])])
                        w2dmk[nmc] = sW2[nmc % 2].n
                    if e_i == 0:
                        p.emit("vector", lambda e, mc=mc, bank=bank: e.tensor_copy(
                            f_acc[:, mc, :], bank.ap()[:, :]),
                            waits=[(sT, mt2)], inc=sV)
                    else:
                        p.emit("vector", lambda e, mc=mc, bank=bank: e.tensor_add(
                            f_acc[:, mc, :], f_acc[:, mc, :], bank.ap()[:, :]),
                            waits=[(sT, mt2)], inc=sV)
                    mevac[mc % 2] = sV.n
            if first_call and tcl == 0:
                tap("facc0", f_acc.rearrange("p a b -> p (a b)"), [(sV, sV.n)])
            rc = slice(tcl * TC, (tcl + 1) * TC)
            for mc in range(8):
                p.emit("vector", lambda e, mc=mc, rc=rc: e.tensor_add(
                    ln_t2F[:, :], r_sb.ap()[:, mc, rc], f_acc[:, mc, :]),
                    waits=([(sS, sS.n), (sV, sV.n)] if mc == 0 else [(sS, sS.n)]),
                    inc=sV)
                mvr = sV.n
                p.emit("scalar", lambda e, mc=mc, rc=rc: e.activation(
                    out=r_sb.ap()[:, mc, rc], in_=ln_t2F[:, :], func=AF.Copy),
                    waits=[(sV, mvr)], inc=sS)


    gate_mark = [0]

    def emit_gating():
        for tcl in range(4):
            bi = tcl % 2
            p.dma("sync", sDL, f_x2c[:, bi, :, :],
                  xn2_hbm.ap().rearrange("(kc k) t -> k kc t", k=128)[
                      :, :, tcl * TC:(tcl + 1) * TC],
                  waits=[(sT, sT.n), (sLN, sLN.n)])
            mld = sDL.n
            for kc in range(8):
                p.emit("tensor", lambda e, kc=kc, bi=bi: e.matmul(
                    ps_e.ap()[0:8, :], gw_sb.ap()[:, kc, :], f_x2c[:, bi, kc, :],
                    start=(kc == 0), stop=(kc == 7)),
                    waits=([(sDL, mld), (sDW, wm["gw"]), (sV, sV.n), (sS, sS.n)]
                           if kc == 0 else []),
                    inc=(sT if kc == 7 else None))
            mt = sT.n
            p.emit("vector", lambda e: e.tensor_scalar(
                ln_tF[0:8, :], ps_e.ap()[0:8, :], gb_sb.ap()[0:8, 0:1], None,
                OP.add, OP.bypass), waits=[(sT, mt), (sDL, wm["gb"])], inc=sV)
            mv = sV.n
            for j in range(4):
                p.emit("tensor", lambda e, j=j: e.transpose(
                    ps_c.ap()[:, j * 8:(j + 1) * 8],
                    ln_tF[0:8, j * 128:(j + 1) * 128], ident_f.ap()[0:8, 0:8]),
                    waits=([(sV, mv)] if j == 0 else []), inc=(sT if j == 3 else None))
            mtr = sT.n
            p.emit("vector", lambda e: e.tensor_copy(
                g_lgT[:, :, :], ps_c.ap()[:, 0:32].rearrange("p (a b) -> p a b", b=8)),
                waits=[(sT, mtr)], inc=sV)
            mg = sV.n
            p.emit("scalar", lambda e: e.activation(
                out=g_ex.rearrange("p a b -> p (a b)"),
                in_=g_lgT.rearrange("p a b -> p (a b)"), func=AF.Exp),
                waits=[(sV, mg)], inc=sS)
            msx = sS.n
            p.emit("vector", lambda e: e.reduce_sum(g_s[:, :], g_ex[:, :, :], axis=AX.X),
                   waits=[(sS, msx)], inc=sV)
            p.emit("vector", lambda e: e.reduce_max(g_m1[:, :], g_lgT[:, :, :], axis=AX.X),
                   inc=sV)
            mgm1 = sV.n
            for j in range(NE):
                p.emit("vector", lambda e, j=j: e.tensor_tensor(
                    g_wk[:, :, j], g_lgT[:, :, j], g_m1[:, :], OP.is_ge),
                    waits=([(sV, mgm1)] if j == 0 else []), inc=sV)
            for j in range(NE):
                p.emit("vector", lambda e, j=j: e.scalar_tensor_tensor(
                    g_l2[:, :, j], g_wk[:, :, j], NEG, g_lgT[:, :, j], OP.mult, OP.add),
                    waits=([(sV, sV.n)] if j == 0 else []), inc=sV)
            ml2 = sV.n
            p.emit("vector", lambda e: e.reduce_max(g_m2[:, :], g_l2[:, :, :], axis=AX.X),
                   waits=[(sV, ml2)], inc=sV)
            mv5 = sV.n
            p.emit("scalar", lambda e: e.activation(out=g_rc[:, :], in_=g_s[:, :],
                                                    func=AF.Ln), waits=[(sV, mv5)], inc=sS)
            p.emit("scalar", lambda e: e.activation(out=g_rc[:, :], in_=g_rc[:, :],
                                                    func=AF.Exp, scale=-1.0),
                   waits=[(sS, sS.n)], inc=sS)
            msr = sS.n
            for j in range(NE):
                p.emit("vector", lambda e, j=j: e.tensor_tensor(
                    g_wk[:, :, j], g_lgT[:, :, j], g_m2[:, :], OP.is_ge),
                    waits=([(sS, msr)] if j == 0 else []), inc=sV)
            for j in range(NE):
                p.emit("vector", lambda e, j=j: e.tensor_mul(
                    g_wk[:, :, j], g_wk[:, :, j], g_ex[:, :, j]),
                    waits=([(sV, sV.n)] if j == 0 else []), inc=sV)
            for j in range(NE):
                p.emit("vector", lambda e, j=j: e.scalar_tensor_tensor(
                    g_wk[:, :, j], g_wk[:, :, j], 1.0, g_rc[:, :], OP.mult, OP.mult),
                    waits=([(sV, sV.n)] if j == 0 else []), inc=sV)
            mw = sV.n
            for j in range(4):
                p.emit("tensor", lambda e, j=j: e.transpose(
                    ps_c.ap()[0:8, j * 128:(j + 1) * 128], g_wk[:, j, :],
                    ident_f.ap()[:, :]),
                    waits=([(sV, mw)] if j == 0 else []), inc=(sT if j == 3 else None))
            mtw = sT.n
            p.emit("vector", lambda e, tcl=tcl: e.tensor_copy(
                g_wfm[0:8, tcl, :], ps_c.ap()[0:8, :]), waits=[(sT, mtw)], inc=sV)
            p.emit("vector", lambda e, tcl=tcl: e.tensor_copy(
                f_wfb[0:8, tcl, :], g_wfm[0:8, tcl, :]),
                waits=[(sV, sV.n)], inc=sV)
        gate_mark[0] = sV.n

    if stage >= 6:
        for tcl in range(4):
            mdone = ln_chunk(tcl, xn3st, ln_tF, ln_t2F)
            if tcl == 0:
                tap("xn20", xn3st.rearrange("p a b -> p (a b)"), [(sV, mdone)])
            p.dma("scalar", sLN,
                  xn2_hbm.ap().rearrange("(kc k) t -> k kc t", k=128)[
                      :, :, tcl * TC:(tcl + 1) * TC],
                  xn3st[:, :, :], waits=[(sV, mdone)])
        emit_ffn_final()
        tap("r_ffn1", r_sb.ap().rearrange("p a b -> p (a b)"), [(sS, sS.n)])
    if stage >= 7:
        for tcl in range(4):
            mdone = ln_chunk(tcl, xn3st, ln_tF, ln_t2F)
            p.dma("scalar", sLN,
                  x3_in.ap().rearrange("(kc k) t -> k kc t", k=128)[
                      :, :, tcl * TC:(tcl + 1) * TC],
                  xn3st[:, :, :], waits=[(sV, mdone)])
        m3s = sLN.n
        p.emit("gpsimd", lambda e: e.collective_compute(
            "AllGather", OP.bypass, replica_groups=GROUPS,
            ins=[x3_in.ap().opt()], outs=[x3_out.ap().opt()]),
            waits=[(sLN, m3s)], inc=sC)
        m3c = sC.n

    if stage >= 8:
        for nm, (dst, src) in {
            "bo": (bo_sb[0:1, :], bo_d[:, :]),
            "bqk": (bqk_sb[0:1, :, :], bqk_d[:, :, :]),
            "bv": (bv_sb[0:1, :], bv_d[:, :]),
        }.items():
            p.dma("gpsimd", sDW, dst, src,
                  waits=[(sT, sT.n), (sV, sV.n), (sS, sS.n)])
        p.dma("gpsimd", sDW, at_wv[:, :, :],
              wv_d.rearrange("(kc k) m -> k kc m", k=128), waits=[(sT, sT.n)])
        mwv = sDW.n
        for nm in ("bo", "bqk", "bv"):
            wm[nm] = mwv

        def x3chunk_ap(j):
            rowbase = (j % 2) * D
            colbase = (j // 2) * TC
            return bass.AP(x3_out, rowbase * TO + colbase,
                           [[TO, 128], [TO * 128, 8], [1, TC]])

        for j in range(8):
            bi = j % 2
            p.dma("sync", sDL, at_x3[:, :, :], x3chunk_ap(j),
                  waits=[(sC, m3c), (sT, sT.n)])
            mld = sDL.n
            for tt in range(4):
                kt = j * 4 + tt
                for kc in range(8):
                    p.emit("tensor", lambda e, kc=kc, bi=bi, tt=tt: e.matmul(
                        ps_c.ap()[:, :], at_x3[:, kc, tt * 128:(tt + 1) * 128],
                        at_wv[:, kc, :], start=(kc == 0), stop=False),
                        waits=([(sDL, mld), (sDW, mwv), (sS, sS.n)] if (tt == 0 and kc == 0)
                               else ([(sS, sS.n)] if kc == 0 else [])),
                        inc=(sT if kc == 7 else None))
                p.emit("tensor", lambda e: e.matmul(
                    ps_c.ap()[:, :], ones_col.ap()[0:1, :], bv_sb[0:1, :],
                    start=False, stop=True), waits=[(sDW, wm["bv"])], inc=sT)
                mt = sT.n
                p.emit("scalar", lambda e, kt=kt: e.activation(
                    out=at_vtok[:, kt, :, 0:64], in_=ps_c.ap()[:, :], func=AF.Copy),
                    waits=[(sT, mt)], inc=sS)
        p.emit("vector", lambda e: e.memset(at_vtok[:, :, :, 64:72], 1.0),
               waits=[(sLN, sLN.n)], inc=sV)
        mvt = sS.n
        mvts = sV.n

        for hp in range(4):
            for qk in range(2):
                p.dma("gpsimd", sDW, at_wqk[:, :, qk, :],
                      bass.AP(wqk_d.tensor, qk * NH * DH + hp * 128,
                              [[2 * NH * DH, 128], [2 * NH * DH * 128, 8], [1, 128]]),
                      waits=[(sT, sT.n)] if qk == 0 else [])
            mwqk = sDW.n
            mva = mvts
            for j in range(8):
                bi = j % 2
                p.dma("sync", sDL, at_x3[:, :, :], x3chunk_ap(j),
                      waits=[(sT, sT.n)])
                mld = sDL.n
                for qk in range(2):
                    psd = ps_c if qk == 0 else ps_d
                    dst = at_q if qk == 0 else at_k
                    for kc in range(8):
                        p.emit("tensor", lambda e, kc=kc, qk=qk, bi=bi, psd=psd: e.matmul(
                            psd.ap()[:, :], at_wqk[:, kc, qk, :], at_x3[:, kc, :],
                            start=(kc == 0), stop=False),
                            waits=([(sDL, mld), (sDW, mwqk), (sV, sV.n), (sS, sS.n)]
                                   if kc == 0 else []),
                            inc=(sT if kc == 7 else None))
                    p.emit("tensor", lambda e, qk=qk, psd=psd, hp=hp: e.matmul(
                        psd.ap()[:, :],
                        bqk_sb.rearrange("p a b -> p (a b)")[0:1, qk * NH * DH + hp * 128:
                                                             qk * NH * DH + hp * 128 + 128],
                        ones_row.ap()[0:1, :], start=False, stop=True),
                        waits=[(sDW, wm["bqk"])], inc=sT)
                    mt = sT.n
                    p.emit("vector", lambda e, j=j, psd=psd, dst=dst: e.tensor_copy(
                        dst[:, j, :], psd.ap()[:, :]), waits=[(sT, mt)], inc=sV)
            mqk = sV.n
            for hh in range(2):
                hl = 2 * hp + hh
                rb = hh * 64
                for qT in range(8):
                    nkt = 4 * (qT + 1)
                    av_ps = ps_c if (qT % 2 == 0) else ps_d
                    npairs = (nkt + 1) // 2
                    for pr in range(npairs):
                        sb = ps2a if (pr % 2 == 0) else ps2b
                        pbuf = pr % 2
                        for sl in range(2):
                            kt = pr * 2 + sl
                            if kt >= nkt:
                                continue
                            dg = kt - 4 * qT
                            p.emit("tensor", lambda e, sb=sb, sl=sl, rb=rb, kt=kt,
                                   qT=qT, dg=dg: e.matmul(
                                       sb.ap()[:, sl * TC:(sl + 1) * TC],
                                       at_k[rb:rb + 64, kt // 4,
                                            (kt % 4) * 128:(kt % 4) * 128 + 128],
                                       at_q[rb:rb + 64, qT, :], start=True, stop=(dg < 0)),
                                   waits=([(sV, mqk), (sS, sS.n)] if sl == 0 else []),
                                   inc=(None if dg >= 0 else
                                        (sT if (sl == 1 or kt == nkt - 1) else None)))
                            if dg >= 0:
                                for z in range(dg):
                                    p.emit("tensor", lambda e, sb=sb, sl=sl, z=z: e.matmul(
                                        sb.ap()[:, sl * TC + z * 128:sl * TC + (z + 1) * 128],
                                        ident_bf.ap()[:, :], negt_bf.ap()[:, :],
                                        start=False, stop=False), inc=None)
                                p.emit("tensor", lambda e, sb=sb, sl=sl, dg=dg: e.matmul(
                                    sb.ap()[:, sl * TC + dg * 128:sl * TC + (dg + 1) * 128],
                                    ident_bf.ap()[:, :], trimask_bf.ap()[:, :],
                                    start=False, stop=True),
                                    inc=(sT if (sl == 1 or kt == nkt - 1) else None))
                        mts = sT.n
                        wid = min(2, nkt - pr * 2)
                        p.emit("scalar", lambda e, sb=sb, pbuf=pbuf, wid=wid: e.activation(
                            out=at_p[:, pbuf, 0:wid, :].rearrange("p a b -> p (a b)"),
                            in_=sb.ap()[:, 0:wid * TC], func=AF.Exp, scale=0.125),
                            waits=[(sT, mts)], inc=sS)
                        msp = sS.n
                        for sl in range(wid):
                            kt = pr * 2 + sl
                            p.emit("tensor", lambda e, av_ps=av_ps, kt=kt, hh=hh, pbuf=pbuf,
                                   sl=sl, nkt=nkt, hl=hl: e.matmul(
                                       av_ps.ap()[0:72, :], at_vtok[:, kt, hl, :],
                                       at_p[:, pbuf, sl, :], start=(kt == 0),
                                       stop=(kt == nkt - 1)),
                                   waits=([(sS, msp), (sV, sV.n)] if kt == 0
                                          else ([(sS, msp)] if sl == 0 else [])),
                                   inc=(sT if kt == nkt - 1 else None))
                    mav = sT.n
                    p.emit("scalar", lambda e, qT=qT, av_ps=av_ps: e.activation(
                        out=at_den[64:72, qT, :], in_=av_ps.ap()[64:72, :], func=AF.Copy),
                        waits=[(sT, mav)], inc=sS)
                    p.emit("vector", lambda e, hh=hh, qT=qT, av_ps=av_ps: e.tensor_copy(
                        at_sg[0:64, hh, qT, :], av_ps.ap()[0:64, :]),
                        waits=[(sT, mav)], inc=sV)
                msg = sV.n
                mden = sS.n
                p.emit("scalar", lambda e: e.activation(
                    out=at_den[64:72, :, :].rearrange("p a b -> p (a b)"),
                    in_=at_den[64:72, :, :].rearrange("p a b -> p (a b)"),
                    func=AF.Ln), waits=[(sS, mden)], inc=sS)
                p.emit("scalar", lambda e: e.activation(
                    out=at_denb[64:72, :, :].rearrange("p a b -> p (a b)"),
                    in_=at_den[64:72, :, :].rearrange("p a b -> p (a b)"),
                    func=AF.Exp, scale=-1.0), waits=[(sS, sS.n)], inc=sS)
                mrc = sS.n
                for qT in range(8):
                    p.emit("tensor", lambda e, qT=qT: e.matmul(
                        ps_e.ap()[0:64, :], rowsel.ap()[64:72, :],
                        at_denb[64:72, qT, :], start=True, stop=True),
                        waits=[(sS, mrc), (sV, sV.n), (sAI, sAI.n)], inc=sT)
                    mre = sT.n
                    p.emit("vector", lambda e, hh=hh, qT=qT: e.tensor_mul(
                        at_onr[0:64, qT % 2, :], at_sg[0:64, hh, qT, :], ps_e.ap()[0:64, :]),
                        waits=[(sT, mre), (sV, msg)], inc=sV)
                    if qT % 2 == 1:
                        mno = sV.n
                        p.dma("scalar", sAI,
                              bass.AP(at_in, hl * 64 * L + (qT - 1) * TC,
                                      [[L, 64], [1, 2 * TC]]),
                              at_onr[0:64, :, :], waits=[(sV, mno)])
        mats = sAI.n
        p.emit("gpsimd", lambda e: e.collective_compute(
            "AllGather", OP.bypass, replica_groups=GROUPS,
            ins=[at_in.ap().opt()], outs=[at_out.ap().opt()]),
            waits=[(sAI, mats)], inc=sC)
        mac = sC.n
        p.dma("gpsimd", sDW, at_wo[:, :, :, :], wo_d[:, :, :, :], waits=[(sT, sT.n)])
        mwo = sDW.n
        for j in range(4):
            ap_at = bass.AP(at_out, regs4[j], [[L, 128], [L * 128, 8], [1, TC]])
            p.dma("sync", sDL, at_at[:, :, :], ap_at, waits=[(sC, mac), (sT, sT.n)])
            mld = sDL.n
            rc = slice(j * TC, (j + 1) * TC)
            for mc in range(8):
                psd = ps_c if mc % 2 == 0 else ps_d
                for hp2 in range(8):
                    p.emit("tensor", lambda e, mc=mc, hp2=hp2, psd=psd: e.matmul(
                        psd.ap()[:, :], at_wo[:, hp2, mc, :], at_at[:, hp2, :],
                        start=(hp2 == 0), stop=False),
                        waits=([(sDL, mld), (sDW, mwo), (sV, sV.n), (sS, sS.n)]
                               if (mc == 0 and hp2 == 0)
                               else ([(sV, sV.n)] if hp2 == 0 else [])),
                        inc=None)
                p.emit("tensor", lambda e, mc=mc, psd=psd: e.matmul(
                    psd.ap()[:, :], bo_sb[0:1, mc * 128:(mc + 1) * 128],
                    ones_row.ap()[0:1, :], start=False, stop=True),
                    waits=[(sDW, wm["bo"])], inc=sT)
                mt = sT.n
                p.emit("vector", lambda e, mc=mc, psd=psd, rc=rc: e.tensor_add(
                    ln_tF[:, :], r_sb.ap()[:, mc, rc], psd.ap()[:, :]),
                    waits=[(sT, mt), (sS, sS.n)], inc=sV)
                mv = sV.n
                p.emit("scalar", lambda e, mc=mc, rc=rc: e.activation(
                    out=r_sb.ap()[:, mc, rc], in_=ln_tF[:, :], func=AF.Copy),
                    waits=[(sV, mv)], inc=sS)
        tap("r_attn", r_sb.ap().rearrange("p a b -> p (a b)"), [(sS, sS.n)])

    if stage >= 9:
        for tcl in range(4):
            mdone = ln_chunk(tcl, xn3st, ln_tF, ln_t2F)
            p.dma("scalar", sLN,
                  xn2_hbm.ap().rearrange("(kc k) t -> k kc t", k=128)[
                      :, :, tcl * TC:(tcl + 1) * TC],
                  xn3st[:, :, :], waits=[(sV, mdone)])
        emit_ffn_final()

    fin_t = vf(4352, D)
    for tt in range(16):
        for half in range(2):
            psd = ps_c if half == 0 else ps_d
            for jj in range(4):
                kc = half * 4 + jj
                p.emit("tensor", lambda e, kc=kc, jj=jj, psd=psd, tt=tt: e.transpose(
                    psd.ap()[:, jj * 128:(jj + 1) * 128],
                    r_sb.ap()[:, kc, tt * 128:(tt + 1) * 128], ident_f.ap()[:, :]),
                    waits=([(sS, sS.n), (sV, sV.n)] if jj == 0 else []),
                    inc=(sT if jj == 3 else None))
            mt = sT.n
            p.emit("vector", lambda e, half=half, psd=psd: e.tensor_copy(
                fin_t[:, half * 512:(half + 1) * 512], psd.ap()[:, :]),
                waits=[(sT, mt), (sO, sO.n)], inc=sV)
        mv = sV.n
        p.dma("scalar", sO, out_d[tt * 128:(tt + 1) * 128, :], fin_t[:, :],
              waits=[(sV, mv)])
    p.emit("sync", None, waits=[(sO, sO.n), (sDW, sDW.n), (sDS, sDS.n),
                                (sY, sY.n), (sAR, sAR.n), (sLN, sLN.n),
                                (sAI, sAI.n), (sWC, sWC.n),
                                (sW1p[0], sW1p[0].n), (sW1p[1], sW1p[1].n),
                                (sW2[0], sW2[0].n), (sW2[1], sW2[1].n)])
    p.finalize()
    return


def _host_inputs(inputs, core):
    g = core % 2
    b = core // 2
    f32 = np.float32
    hs = np.asarray(inputs["L0_attn_a_hidden_states"][b], dtype=f32)
    own_rows = np.concatenate([np.arange((2 * j + g) * 512, (2 * j + g) * 512 + 512)
                               for j in range(4)])
    ch0 = g * DIH
    ipw = np.asarray(inputs["L0_attn_a_in_proj_w"], dtype=f32)
    d = {
        "hid": np.ascontiguousarray(hs),
        "hidown": np.ascontiguousarray(hs[own_rows]),
        "gofs": np.asarray([[g * 512 + j * 1024 for j in range(4)]], dtype=np.int32),
        "inxw": np.ascontiguousarray(ipw[ch0:ch0 + DIH, :].T),
        "inzw": np.ascontiguousarray(ipw[2048 + ch0:2048 + ch0 + DIH, :].T),
        "convw": np.ascontiguousarray(
            np.asarray(inputs["L0_attn_a_conv_w"], dtype=f32)[ch0:ch0 + DIH, 0, :]
            .reshape(8, 128, 4).transpose(1, 0, 2)),
        "convb": np.ascontiguousarray(
            np.asarray(inputs["L0_attn_a_conv_b"], dtype=f32)[ch0:ch0 + DIH]
            .reshape(8, 128).T),
        "xpw": np.ascontiguousarray(
            np.asarray(inputs["L0_attn_a_x_proj_w"], dtype=f32)[:, ch0:ch0 + DIH].T),
        "dtw": np.ascontiguousarray(
            np.asarray(inputs["L0_attn_a_dt_w"], dtype=f32)[ch0:ch0 + DIH, :].T),
        "dtb": np.ascontiguousarray(
            np.asarray(inputs["L0_attn_a_dt_b"], dtype=f32)[ch0:ch0 + DIH]
            .reshape(8, 128).T),
        "alog": np.ascontiguousarray(
            np.asarray(inputs["L0_attn_a_A_log"], dtype=f32)[ch0:ch0 + DIH]
            .reshape(8, 128, NST).transpose(1, 0, 2).reshape(128, 8 * NST)),
        "dvec": np.ascontiguousarray(
            np.asarray(inputs["L0_attn_a_D"], dtype=f32)[ch0:ch0 + DIH]
            .reshape(8, 128).T),
        "opw": np.ascontiguousarray(
            np.asarray(inputs["L0_attn_a_out_proj_w"], dtype=f32).T),
        "wo": np.ascontiguousarray(
            np.asarray(inputs["L0_attn_b_Wo"], dtype=f32)
            .reshape(8, 128, 8, 128).transpose(1, 0, 2, 3)),
        "bo": np.asarray(inputs["L0_attn_b_bo"], dtype=f32).reshape(1, D),
        "gw": np.ascontiguousarray(np.asarray(inputs["L0_ffn_gate_w"], dtype=f32)),
        "gb": np.asarray(inputs["L0_ffn_gate_b"], dtype=f32).reshape(NE, 1),
        "w1": np.ascontiguousarray(np.asarray(inputs["L0_ffn_w1"], dtype=f32)),
        "b1": np.ascontiguousarray(
            np.asarray(inputs["L0_ffn_b1"], dtype=f32).reshape(NE, 16, 128)
            .transpose(2, 0, 1)),
        "w2": np.ascontiguousarray(np.asarray(inputs["L0_ffn_w2"], dtype=f32)),
        "b2": np.ascontiguousarray(np.asarray(inputs["L0_ffn_b2"], dtype=f32)),
    }
    h0 = g * NH
    wq = np.asarray(inputs["L0_attn_b_Wq"], dtype=f32)[h0:h0 + NH]
    wk = np.asarray(inputs["L0_attn_b_Wk"], dtype=f32)[h0:h0 + NH]
    wv = np.asarray(inputs["L0_attn_b_Wv"], dtype=f32)[h0:h0 + NH]
    d["wqk"] = np.ascontiguousarray(np.stack(
        [wq.transpose(1, 0, 2).reshape(D, NH * DH),
         wk.transpose(1, 0, 2).reshape(D, NH * DH)], axis=1))
    d["wv"] = np.ascontiguousarray(wv.transpose(1, 0, 2).reshape(D, NH * DH))
    bq = np.asarray(inputs["L0_attn_b_bq"], dtype=f32)[h0:h0 + NH].reshape(NH * DH)
    bk = np.asarray(inputs["L0_attn_b_bk"], dtype=f32)[h0:h0 + NH].reshape(NH * DH)
    d["bqk"] = np.ascontiguousarray(np.stack([bq, bk])[None, :, :])
    d["bv"] = np.ascontiguousarray(
        np.asarray(inputs["L0_attn_b_bv"], dtype=f32)[h0:h0 + NH].reshape(1, NH * DH))
    return d


def _build_nc(dbg=None, stage=99, ffn_lim=(4, 8)):
    nc = bass.Bass("TRN2", target_bir_lowering=False)
    build(nc, dbg=dbg, stage=stage, ffn_lim=ffn_lim)
    return nc


def kernel(**inputs):
    nc = _build_nc()
    in_maps = [_host_inputs(inputs, core) for core in range(8)]
    res = run_bass_kernel_spmd(nc, in_maps, core_ids=list(range(8)))
    out = np.zeros((4, L, D), dtype=np.float32)
    for core in range(8):
        g = core % 2
        b = core // 2
        o = np.asarray(res.results[core]["out"])
        for j in range(4):
            gt = (2 * j + g) * 512
            out[b, gt:gt + 512, :] = o[j * 512:(j + 1) * 512, :]
    return out


# version 05
# speedup vs baseline: 1.0664x; 1.0664x over previous
import numpy as np
import concourse.bass as bass
import concourse.mybir as mybir
from concourse.bass_utils import run_bass_kernel_spmd

F32 = mybir.dt.float32
BF16 = mybir.dt.bfloat16
I32 = mybir.dt.int32
AF = mybir.ActivationFunctionType
OP = mybir.AluOpType
AX = mybir.AxisListType
ET = mybir.EngineType

L = 4096
D = 1024
TO = 2048
DIH = 1024
NST = 16
NH = 8
DH = 64
NE = 8
HF = 2048
TC = 512
T2 = 256
SC = 1024
NSC = L // SC
NEG = -1.0e30
EPS = 1e-5

ENGINES = ("sync", "tensor", "vector", "scalar", "gpsimd")
CONSTVALS = [EPS, 1.0, -0.5, -1.0, 0.125, 0.0, 2.0, -2.0]


class Sem:
    def __init__(self, nc, name):
        self.h = nc.alloc_semaphore(name)
        self.n = 0


class Prog:
    def __init__(self, nc):
        self.nc = nc
        self.steps = {e: [] for e in ENGINES}

    def emit(self, eng, fn=None, waits=(), inc=None, inck=1):
        w = [(s.h, int(v)) for (s, v) in waits]
        ih = None
        if inc is not None:
            inc.n += inck
            ih = inc.h
        self.steps[eng].append((fn, w, ih, inck))

    def dma(self, eng, sem, out, in_, waits=()):
        self.emit(eng, lambda e, o=out, i=in_: e.dma_start(out=o, in_=i),
                  waits=waits, inc=sem, inck=16)

    def finalize(self):
        with self.nc.Block() as blk:
            def run(eng_name):
                def body(eng):
                    for fn, waits, ih, ik in self.steps[eng_name]:
                        for (h, v) in waits:
                            eng.wait_ge(h, v)
                        instr = fn(eng) if fn is not None else None
                        if ih is not None:
                            if instr is None:
                                instr = eng.nop()
                            instr.then_inc(ih, ik)
                return body
            blk.sync(run("sync"))
            blk.tensor(run("tensor"))
            blk.vector(run("vector"))
            blk.scalar(run("scalar"))
            blk.gpsimd(run("gpsimd"))


def build(nc, dbg=None, stage=99, ffn_lim=(4, 8)):
    dbg = dict(dbg or {})
    p = Prog(nc)

    def din(name, shape, dt=F32):
        return nc.dram_tensor(name, list(shape), dt, kind="ExternalInput")

    hid = din("hid", [L, D]).ap()
    hidown = din("hidown", [TO, D]).ap()
    gofs_d = din("gofs", [1, 4], I32).ap()
    inxw_d = din("inxw", [D, DIH]).ap()
    inzw_d = din("inzw", [D, DIH]).ap()
    convw_d = din("convw", [128, 8, 4]).ap()
    convb_d = din("convb", [128, 8]).ap()
    xpw_d = din("xpw", [DIH, 96]).ap()
    dtw_d = din("dtw", [64, DIH]).ap()
    dtb_d = din("dtb", [128, 8]).ap()
    alog_d = din("alog", [128, 8 * NST]).ap()
    dvec_d = din("dvec", [128, 8]).ap()
    opw_d = din("opw", [2 * DIH, D]).ap()
    wqk_d = din("wqk", [D, 2, NH * DH]).ap()
    wv_d = din("wv", [D, NH * DH]).ap()
    bqk_d = din("bqk", [1, 2, NH * DH]).ap()
    bv_d = din("bv", [1, NH * DH]).ap()
    wo_d = din("wo", [128, 8, 8, 128]).ap()
    bo_d = din("bo", [1, D]).ap()
    gw_d = din("gw", [D, NE]).ap()
    gb_d = din("gb", [NE, 1]).ap()
    w1_d = din("w1", [NE, D, HF]).ap()
    b1_d = din("b1", [128, NE, HF // 128]).ap()
    w2_d = din("w2", [NE, HF, D]).ap()
    w1b_d = nc.dram_tensor("w1b", [NE, D, HF], BF16).ap()
    w2b_d = nc.dram_tensor("w2b", [NE, HF, D], BF16).ap()
    b2_d = din("b2", [NE, D]).ap()

    out_d = nc.dram_tensor("out", [TO, D], F32, kind="ExternalOutput").ap()
    dbg_d = {}
    for name, shape in dbg.items():
        dbg_d[name] = nc.dram_tensor("dbg_" + name, list(shape), F32,
                                     kind="ExternalOutput").ap()

    xp_in = [nc.dram_tensor(f"xp_in{s}", [96, SC], F32) for s in range(NSC)]
    xp_out = [nc.dram_tensor(f"xp_out{s}", [96, SC], F32) for s in range(NSC)]
    y_in = [nc.dram_tensor(f"y_in{s}", [DIH, SC], BF16) for s in range(NSC)]
    y_out = [nc.dram_tensor(f"y_out{s}", [2 * DIH, SC], BF16) for s in range(NSC)]
    x3_in = nc.dram_tensor("x3_in", [D, TO], BF16)
    x3_out = nc.dram_tensor("x3_out", [2 * D, TO], BF16)
    at_in = nc.dram_tensor("at_in", [NH * DH, L], BF16)
    at_out = nc.dram_tensor("at_out", [2 * NH * DH, L], BF16)
    xn2_hbm = nc.dram_tensor("xn2_hbm", [D, TO], BF16)
    GROUPS = [[0, 1], [2, 3], [4, 5], [6, 7]]

    def A(name, shape, dt):
        return nc.alloc_sbuf_tensor(name, list(shape), dt)

    r_sb = A("r_sb", [128, 8, TO], F32)
    states = A("states", [128, 128], F32)
    ident_bf = A("ident_bf", [128, 128], BF16)
    ident_f = A("ident_f", [128, 128], F32)
    trimask_bf = A("trimask_bf", [128, 128], BF16)
    negt_bf = A("negt_bf", [128, 128], BF16)
    ones_row = A("ones_row", [1, TC], BF16)
    ones_col = A("ones_col", [1, 128], BF16)
    ones128 = A("ones128", [128, 1], BF16)
    rowsel = A("rowsel", [128, DH], BF16)
    esel = A("esel", [NE, NE, 128], BF16)
    a_sb = A("a_sb", [128, 8 * NST], F32)
    alog_sb = A("alog_sb", [128, 8 * NST], F32)
    convw_sb = A("convw_sb", [128, 8, 4], F32)
    convb_sb = A("convb_sb", [128, 8], F32)
    dtb_sb = A("dtb_sb", [128, 8], F32)
    dvec_sb = A("dvec_sb", [128, 8], F32)
    gb_sb = A("gb_sb", [NE, 1], F32)
    b1_sb = A("b1_sb", [128, NE, HF // 128], F32)
    b2_sb = A("b2_sb", [NE, NE, 128], BF16)
    gw_sb = A("gw_sb", [128, 8, NE], BF16)
    gofs_sb = A("gofs_sb", [1, 4], I32)
    cvals = A("cvals", [128, 8], F32)

    AF32 = A("AF32", [128, 10240], F32)
    AB16 = A("AB16", [128, 46592], BF16)
    af = AF32.ap()
    ab = AB16.ap()

    def _view(apx, shape):
        if len(shape) == 1:
            return apx
        names = [f"a{i}" for i in range(len(shape))]
        kw = {nm: s for nm, s in zip(names[1:], shape[1:])}
        return apx.rearrange("p (" + " ".join(names) + ") -> p " + " ".join(names), **kw)

    def vf(off, *shape):
        n = int(np.prod(shape))
        assert off + n <= 10240, ("vf", off, shape)
        return _view(af[:, off:off + n], shape)

    def vb(off, *shape):
        n = int(np.prod(shape))
        assert off + n <= 46592, ("vb", off, shape)
        return _view(ab[:, off:off + n], shape)

    inxw = vb(0, 8, 8, 128)
    inzw = vb(8192, 8, 8, 128)
    opw = vb(0, 16, 8, 128)
    xpw = vb(16384, 8, 96)
    dtw = vb(17152, 8, 128)
    u_bf = vb(18176, 8, SC)
    yfull = vb(18176, 16, TC)
    sz_bf = vb(26368, 8, SC)
    xp_dtb = vb(44832, SC)
    xp_bcb = vb(34560, 4, T2)
    xin = vb(35584, 8, 4 + TC)
    xn1 = vb(39712, 8, TC)
    m2_hC = vb(39712, 2, T2)
    m2_y2 = vb(40224, 8, T2)
    xn1_tm = vb(43808, D)
    f_w1e = vb(0, 8, 16, 128)
    f_w2c = vb(16384, 2, 16, 128)
    f_hp = vb(20480, 16, TC)
    f_x2c = vb(28672, 2, 8, TC)
    f_wr = vb(36864, TC)
    f_wfb = vb(37376, 4, TC)
    stb = vb(39424, 1024)
    f_lbf = vb(20480, 8, TC)
    f_lsq = vb(24576, 8, TC)
    f_rel = vb(40448, 2, TC)
    xn3st = vb(16384, 8, TC)
    at_vtok = vb(0, 32, NH, 72)
    at_q = vb(18432, 8, TC)
    at_k = vb(22528, 8, TC)
    at_p = vb(26624, 2, 2, TC)
    at_x3 = vb(28672, 8, TC)
    at_onr = vb(32768, 2, TC)
    bo_sb = vb(33792, D)
    bqk_sb = vb(34816, 2, NH * DH)
    bv_sb = vb(35840, NH * DH)
    at_wqk = vb(36352, 8, 2, 128)
    at_wv = vb(38400, 8, TC)
    at_sg = vb(38400, 2, 8, TC)
    at_denb = vb(28672, 8, TC)
    at_wo = vb(18432, 8, 8, 128)
    at_at = vb(26624, 8, TC)

    xp_part = vf(0, SC)
    m1_xall = vf(2048, D)
    m1_xsq = vf(3072, D)
    m1_cva = vf(4096, TC)
    m1_cvb = vf(4608, TC)
    m1_st_t = A("m1_st_t", [128, 16], F32)
    m1_st = m1_st_t.ap()
    delta = vf(2048, 8, T2)
    sp_v = vf(4096, T2)
    sp_a = vf(4352, T2)
    sp_b = vf(4608, T2)
    dA_b = vf(4864, 2, T2)
    dBu_b = vf(5376, 2, T2)
    h_b = vf(5888, 2, T2)
    du_all = vf(6400, 8, T2)
    brepB = vf(8448, T2)
    brepC = vf(8704, T2)
    t1_sb = vf(8960, T2)
    ln_t = vf(9216, TC)
    ln_t2 = vf(9728, TC)
    g_lgT = vf(0, 4, NE)
    g_ex = vf(32, 4, NE)
    g_l2 = vf(64, 4, NE)
    g_wk = vf(96, 4, NE)
    g_s = vf(128, 4)
    g_m1 = vf(132, 4)
    g_m2 = vf(136, 4)
    g_rc = vf(140, 4)
    g_wfm = vf(256, 4, TC)
    ln_tF = vf(2304, TC)
    ln_t2F = vf(2560, TC)
    stf = vf(2816, 1536)
    at_den = vf(0, 8, TC)

    ps2a = nc.alloc_psum_tensor("ps2a", [128, 1024], F32)
    ps2b = nc.alloc_psum_tensor("ps2b", [128, 1024], F32)
    ps_c = nc.alloc_psum_tensor("ps_c", [128, 512], F32)
    ps_d = nc.alloc_psum_tensor("ps_d", [128, 512], F32)
    ps_e = nc.alloc_psum_tensor("ps_e", [128, 512], F32)
    ps_h = nc.alloc_psum_tensor("ps_h", [128, 512], F32)
    psT = ps_h.ap().bitcast(BF16)
    psm = [ps2a.ap()[:, 0:512], ps2a.ap()[:, 512:1024],
           ps2b.ap()[:, 0:512], ps2b.ap()[:, 512:1024],
           ps_c.ap()[:, :], ps_d.ap()[:, :], ps_e.ap()[:, :], ps_h.ap()[:, :]]

    sDW = Sem(nc, "sDW")
    sDL = Sem(nc, "sDL")
    sDS = Sem(nc, "sDS")
    sT = Sem(nc, "sT")
    sV = Sem(nc, "sV")
    sS = Sem(nc, "sS")
    sG = Sem(nc, "sG")
    sC = Sem(nc, "sC")
    sO = Sem(nc, "sO")
    sY = Sem(nc, "sY")
    sAR = Sem(nc, "sAR")
    sLN = Sem(nc, "sLN")
    sAI = Sem(nc, "sAI")
    sW1 = Sem(nc, "sW1")
    sW2 = [Sem(nc, "sW2a"), Sem(nc, "sW2b")]
    sWC = Sem(nc, "sWC")

    def tap(name, src_ap, waits):
        if name in dbg_d:
            p.dma("gpsimd", sDW, dbg_d[name][:, :], src_ap, waits=waits)
            for eng in ENGINES:
                p.emit(eng, None, waits=[(sDW, sDW.n)])

    wm = {}
    for nm, (dst, src) in {
        "inxw": (inxw[:, :, :, :], inxw_d.rearrange("(kc k) (mc m) -> k kc mc m", k=128, m=128)),
        "inzw": (inzw[:, :, :, :], inzw_d.rearrange("(kc k) (mc m) -> k kc mc m", k=128, m=128)),
        "xpw": (xpw[:, :, :], xpw_d.rearrange("(kc k) m -> k kc m", k=128)),
        "dtw": (dtw[0:64, :, :], dtw_d.rearrange("k (mc m) -> k mc m", m=128)),
        "b2": (b2_sb[:, :, :], b2_d.rearrange("e (mc m) -> e mc m", m=128)),
        "gw": (gw_sb[:, :, :], gw_d.rearrange("(kc k) m -> k kc m", k=128)),
    }.items():
        p.dma("gpsimd", sDW, dst, src)
    for nm in ("inxw", "inzw", "xpw", "dtw", "b2", "gw"):
        wm[nm] = sDW.n
    p.dma("sync", sDL, gofs_sb[:, :], gofs_d[:, :])
    for nm, (dst, src) in {
        "convw": (convw_sb[:, :, :], convw_d[:, :, :]),
        "convb": (convb_sb[:, :], convb_d[:, :]),
        "dtb": (dtb_sb[:, :], dtb_d[:, :]),
        "dvec": (dvec_sb[:, :], dvec_d[:, :]),
        "gb": (gb_sb[:, :], gb_d[:, :]),
        "b1": (b1_sb[:, :, :], b1_d[:, :, :]),
        "alog": (alog_sb[:, :], alog_d[:, :]),
    }.items():
        p.dma("sync", sDL, dst, src)
    for nm in ("convw", "convb", "dtb", "dvec", "gb", "b1", "alog"):
        wm[nm] = sDL.n
    for e_ in range(NE):
        p.dma("gpsimd", sWC, w1b_d[e_], w1_d[e_])
        p.dma("gpsimd", sWC, w2b_d[e_], w2_d[e_])
    mcast = sWC.n
    m0 = sDL.n
    regs4 = [nc.alloc_register(ET.SP, f"gofs{j}") for j in range(4)]
    for j in range(4):
        p.emit("sync", lambda e, j=j: e.reg_load(regs4[j], gofs_sb[0:1, j:j + 1]),
               waits=([(sDL, m0)] if j == 0 else []))

    def _cpair(mfn, afn):
        p.emit("gpsimd", mfn, inc=sG)
        mk = sG.n
        p.emit("gpsimd", afn, waits=[(sG, mk)], inc=sG)

    _cpair(lambda e: e.memset(ident_bf[:], 1.0),
           lambda e: e.affine_select(ident_bf[:], ident_bf[:], [[1, 128]],
                                     OP.is_equal, 0.0, channel_multiplier=-1))
    _cpair(lambda e: e.memset(ident_f[:], 1.0),
           lambda e: e.affine_select(ident_f[:], ident_f[:], [[1, 128]],
                                     OP.is_equal, 0.0, channel_multiplier=-1))
    _cpair(lambda e: e.memset(trimask_bf[:], 0.0),
           lambda e: e.affine_select(trimask_bf[:], trimask_bf[:], [[1, 128]],
                                     OP.is_ge, NEG, channel_multiplier=-1))
    _cpair(lambda e: e.memset(rowsel[:], 1.0),
           lambda e: e.affine_select(rowsel[:], rowsel[:], [[0, DH]],
                                     OP.is_equal, 0.0, base=-64,
                                     channel_multiplier=1))
    _cpair(lambda e: e.memset(esel[:], 1.0),
           lambda e: e.affine_select(esel[:], esel[:], [[1, NE], [0, 128]],
                                     OP.is_equal, 0.0, channel_multiplier=-1))

    for dst_, val_ in [(negt_bf, NEG), (ones_row, 1.0), (ones_col, 1.0),
                       (ones128, 1.0)]:
        p.emit("gpsimd", lambda e, d=dst_, v=val_: e.memset(d[:], v), inc=sG)
    for col, val in enumerate(CONSTVALS):
        p.emit("gpsimd", lambda e, col=col, val=val: e.memset(
            cvals[:, col:col + 1], val), inc=sG)
    gC = sG.n
    p.emit("vector", lambda e: e.memset(xin[:, :, 0:4], 0.0), inc=sV)
    for col, val in enumerate(CONSTVALS):
        nc.const_aps.register(F32, val, cvals.ap()[:, col:col + 1])
    p.emit("scalar", None, waits=[(sG, gC)])
    p.emit("vector", None, waits=[(sG, gC)])
    p.emit("vector", lambda e: e.memset(states[:], 0.0), inc=sV)
    p.emit("scalar", lambda e: e.activation(out=a_sb[:], in_=alog_sb[:], func=AF.Exp),
           waits=[(sDL, wm["alog"])], inc=sS)
    mA = sS.n
    p.emit("vector", lambda e: e.tensor_scalar_mul(a_sb[:], a_sb[:], -1.0),
           waits=[(sS, mA)], inc=sV)
    mAv = sV.n

    for tt in range(16):
        p.dma("sync", sDL, m1_xall[:, :], hidown[tt * 128:(tt + 1) * 128, :],
              waits=([(sDL, m0), (sT, sT.n)] if tt == 0 else [(sT, sT.n)]))
        mload = sDL.n
        for half in range(2):
            psd = ps_c if half == 0 else ps_d
            for j in range(4):
                kc = half * 4 + j
                p.emit("tensor", lambda e, kc=kc, j=j, psd=psd: e.transpose(
                    psd.ap()[:, j * 128:(j + 1) * 128],
                    m1_xall[:, kc * 128:(kc + 1) * 128], ident_f.ap()[:, :]),
                    waits=([(sDL, mload), (sG, gC), (sV, sV.n)] if j == 0 else []),
                    inc=(sT if j == 3 else None))
            mt = sT.n
            p.emit("vector", lambda e, half=half, tt=tt, psd=psd: e.tensor_copy(
                r_sb.ap()[:, half * 4:half * 4 + 4, tt * 128:(tt + 1) * 128],
                psd.ap()[:, 0:512].rearrange("p (a b) -> p a b", b=128)),
                waits=[(sT, mt)], inc=sV)

    scan_marks = []

    def m1_chunk(tc):
        t0 = tc * TC
        scq = tc % (SC // TC)
        ucols = slice(scq * TC, (scq + 1) * TC)
        for tt in range(4):
            p.dma("sync", sDL, m1_xall[:, :], hid[t0 + tt * 128:t0 + (tt + 1) * 128, :],
                  waits=[(sS, sS.n), (sT, sT.n), (sV, sV.n)])
            mload = sDL.n
            p.emit("vector", lambda e: e.reduce_sum(m1_st[:, 0:1], m1_xall[:, :], axis=AX.X),
                   waits=[(sDL, mload)], inc=sV)
            p.emit("scalar", lambda e: e.activation(out=m1_xsq[:, :], in_=m1_xall[:, :],
                                                    func=AF.Square),
                   waits=[(sDL, mload), (sV, sV.n)], inc=sS)
            msq = sS.n
            p.emit("vector", lambda e: e.reduce_sum(m1_st[:, 1:2], m1_xsq[:, :], axis=AX.X),
                   waits=[(sS, msq)], inc=sV)
            p.emit("vector", lambda e: e.tensor_scalar_mul(m1_st[:, 2:3], m1_st[:, 0:1], 1.0 / D),
                   waits=[(sV, sV.n)], inc=sV)
            p.emit("vector", lambda e: e.tensor_scalar_mul(m1_st[:, 3:4], m1_st[:, 1:2], 1.0 / D),
                   inc=sV)
            p.emit("vector", lambda e: e.tensor_mul(m1_st[:, 4:5], m1_st[:, 2:3], m1_st[:, 2:3]),
                   waits=[(sV, sV.n)], inc=sV)
            p.emit("vector", lambda e: e.tensor_sub(m1_st[:, 5:6], m1_st[:, 3:4], m1_st[:, 4:5]),
                   waits=[(sV, sV.n)], inc=sV)
            mv = sV.n
            p.emit("scalar", lambda e: e.activation(out=m1_st[:, 6:7], in_=m1_st[:, 5:6],
                                                    func=AF.Ln, bias=EPS),
                   waits=[(sV, mv)], inc=sS)
            p.emit("scalar", lambda e: e.activation(out=m1_st[:, 7:8], in_=m1_st[:, 6:7],
                                                    func=AF.Exp, scale=-0.5),
                   waits=[(sS, sS.n)], inc=sS)
            ms2 = sS.n
            p.emit("vector", lambda e: e.scalar_tensor_tensor(
                m1_st[:, 8:9], m1_st[:, 2:3], -1.0, m1_st[:, 7:8], OP.mult, OP.mult),
                waits=[(sS, ms2)], inc=sV)
            mv2 = sV.n
            p.emit("vector", lambda e: e.tensor_scalar(
                xn1_tm[:, :], m1_xall[:, :], m1_st[:, 7:8], m1_st[:, 8:9],
                OP.mult, OP.add), waits=[(sV, mv2)], inc=sV)
            ms3 = sV.n
            if tc == 0 and tt == 0:
                tap("xall0", m1_xall[:, :], [(sV, ms3)])
                tap("m1st0", m1_st[:, :], [(sV, ms3)])
                tap("xn1tm0", None, []) if False else None
                tap("xntm0", xn1_tm[:, :], [(sV, ms3)])
            for grp in range(2):
                for j in range(4):
                    jj = grp * 4 + j
                    p.emit("tensor", lambda e, jj=jj, j=j: e.transpose(
                        psT[:, j * 128:(j + 1) * 128],
                        xn1_tm[:, jj * 128:(jj + 1) * 128], ident_bf.ap()[:, :]),
                        waits=([(sV, sV.n)] if j == 0 else []),
                        inc=(sT if j == 3 else None))
                mtt = sT.n
                p.emit("vector", lambda e, grp=grp, tt=tt: e.tensor_copy(
                    xn1[:, grp * 4:grp * 4 + 4, tt * 128:(tt + 1) * 128],
                    psT[:, 0:512].rearrange("p (a b) -> p a b", b=128)),
                    waits=([(sT, mtt), (sY, sY.n)] if (tt == 0 and grp == 0)
                           else [(sT, mtt)]), inc=sV)
        mxn1 = sV.n
        if tc == 0:
            tap("xn10", xn1.rearrange("p a b -> p (a b)"), [(sV, mxn1)])
        for part in range(2):
            ww = inxw if part == 0 else inzw
            for mc in range(8):
                psd = ps_c if (mc % 2 == 0) else ps_d
                w0 = [(sV, mxn1), (sDW, wm["inzw"])] if (part == 0 and mc == 0) else []
                w0 = w0 + [(sS, sS.n)]
                for kc in range(8):
                    p.emit("tensor", lambda e, ww=ww, mc=mc, kc=kc, psd=psd: e.matmul(
                        psd.ap()[:, :], ww[:, kc, mc, :], xn1[:, kc, :],
                        start=(kc == 0), stop=(kc == 7)),
                        waits=(w0 if kc == 0 else []),
                        inc=(sT if kc == 7 else None))
                mt = sT.n
                if part == 0:
                    p.emit("scalar", lambda e, mc=mc, psd=psd: e.activation(
                        out=xin[:, mc, 4:4 + TC], in_=psd.ap()[:, :], func=AF.Copy),
                        waits=[(sT, mt), (sV, sV.n)], inc=sS)
                else:
                    p.emit("scalar", lambda e, mc=mc, psd=psd, ucols=ucols: e.activation(
                        out=sz_bf[:, mc, ucols], in_=psd.ap()[:, :], func=AF.Silu),
                        waits=[(sT, mt)], inc=sS)
        for mc in range(8):
            ms = sS.n
            p.emit("vector", lambda e, mc=mc: e.tensor_scalar(
                m1_cva[:, :], xin[:, mc, 1:1 + TC], convw_sb.ap()[:, mc, 0:1],
                convb_sb.ap()[:, mc:mc + 1], OP.mult, OP.add),
                waits=[(sS, ms), (sDL, wm["convb"])], inc=sV)
            p.emit("vector", lambda e, mc=mc: e.scalar_tensor_tensor(
                m1_cvb[:, :], xin[:, mc, 2:2 + TC], convw_sb.ap()[:, mc, 1:2],
                m1_cva[:, :], OP.mult, OP.add), waits=[(sV, sV.n)], inc=sV)
            p.emit("vector", lambda e, mc=mc: e.scalar_tensor_tensor(
                m1_cva[:, :], xin[:, mc, 3:3 + TC], convw_sb.ap()[:, mc, 2:3],
                m1_cvb[:, :], OP.mult, OP.add), waits=[(sV, sV.n)], inc=sV)
            p.emit("vector", lambda e, mc=mc: e.scalar_tensor_tensor(
                m1_cvb[:, :], xin[:, mc, 4:4 + TC], convw_sb.ap()[:, mc, 3:4],
                m1_cva[:, :], OP.mult, OP.add), waits=[(sV, sV.n)], inc=sV)
            mcv = sV.n
            p.emit("scalar", lambda e, mc=mc, ucols=ucols: e.activation(
                out=u_bf[:, mc, ucols], in_=m1_cvb[:, :], func=AF.Silu),
                waits=[(sV, mcv)], inc=sS)
        msu = sS.n
        p.emit("vector", lambda e: e.tensor_copy(xin[:, :, 1:4], xin[:, :, 513:516]),
               waits=[(sV, sV.n), (sS, msu)], inc=sV)
        for kc in range(8):
            p.emit("tensor", lambda e, kc=kc, ucols=ucols: e.matmul(
                ps_e.ap()[0:96, :], xpw[:, kc, :], u_bf[:, kc, ucols],
                start=(kc == 0), stop=(kc == 7)),
                waits=([(sS, msu), (sV, sV.n)] if kc == 0 else []),
                inc=(sT if kc == 7 else None))
        mt = sT.n
        p.emit("vector", lambda e, ucols=ucols: e.tensor_copy(
            xp_part[0:96, ucols], ps_e.ap()[0:96, :]),
            waits=[(sT, mt), (sAR, sAR.n)], inc=sV)

    def m2_chunk(sc, q, mxpl):
        cols = slice(q * T2, (q + 1) * T2)
        p.dma("gpsimd", sDW, xp_bcb[0:8, :, :],
              xp_out[sc].ap()[64:96, q * T2:(q + 1) * T2].rearrange(
                  "(blk pp) t -> pp blk t", pp=8),
              waits=([(sT, sT.n), (sDW, mxpl)] if q == 0 else [(sT, sT.n)]))
        mbcl = sDW.n
        for mc in range(8):
            psd = ps_c if mc % 2 == 0 else ps_d
            p.emit("tensor", lambda e, mc=mc, psd=psd, cols=cols: e.matmul(
                psd.ap()[:, 0:T2], dtw[0:64, mc, :], xp_dtb[0:64, cols],
                start=True, stop=True),
                waits=[(sV, sV.n), (sS, sS.n), (sDW, sDW.n)], inc=sT)
            mt = sT.n
            p.emit("vector", lambda e, psd=psd, mc=mc: e.tensor_scalar(
                sp_v[:, :], psd.ap()[:, 0:T2], dtb_sb.ap()[:, mc:mc + 1], None,
                OP.add, OP.bypass),
                waits=[(sT, mt)], inc=sV)
            mv = sV.n
            p.emit("scalar", lambda e: e.activation(out=sp_a[:, :], in_=sp_v[:, :],
                                                    func=AF.Abs), waits=[(sV, mv)], inc=sS)
            p.emit("scalar", lambda e: e.activation(out=sp_b[:, :], in_=sp_a[:, :],
                                                    func=AF.Exp, scale=-1.0),
                   waits=[(sS, sS.n)], inc=sS)
            p.emit("scalar", lambda e: e.activation(out=sp_a[:, :], in_=sp_b[:, :],
                                                    func=AF.Ln, bias=1.0),
                   waits=[(sS, sS.n)], inc=sS)
            ms = sS.n
            p.emit("vector", lambda e, mc=mc: e.scalar_tensor_tensor(
                delta[:, mc, :], sp_v[:, :], 0.0, sp_a[:, :], OP.max, OP.add),
                waits=[(sS, ms)], inc=sV)
        for dt in range(8):
            p.emit("vector", lambda e, dt=dt, cols=cols: e.tensor_mul(
                du_all[:, dt, :], delta[:, dt, :], u_bf[:, dt, cols]),
                waits=([(sV, sV.n)] if dt == 0 else []), inc=sV)
        mdu = sV.n
        if sc == 0 and q == 0:
            tap("asb", a_sb.ap()[:, :], [(sV, mdu)])
            tap("delta0", delta.rearrange("p a b -> p (a b)"), [(sV, mdu), (sS, sS.n)])
            tap("du0", du_all.rearrange("p a b -> p (a b)"), [(sV, mdu)])
        for n in range(NST):
            jB, jC = n, 16 + n
            p.emit("tensor", lambda e, jB=jB: e.matmul(
                ps_e.ap()[:, 0:T2], esel.ap()[0:8, jB % 8, :],
                xp_bcb[0:8, jB // 8, :], start=True, stop=False),
                waits=[(sV, mdu), (sS, sS.n), (sDW, mbcl)], inc=sT)
            p.emit("tensor", lambda e, jC=jC: e.matmul(
                ps_e.ap()[:, T2:2 * T2], esel.ap()[0:8, jC % 8, :],
                xp_bcb[0:8, jC // 8, :], start=False, stop=True), inc=sT)
            mt = sT.n
            p.emit("scalar", lambda e: e.activation(out=brepB[:, :], in_=ps_e.ap()[:, 0:T2],
                                                    func=AF.Copy), waits=[(sT, mt)], inc=sS)
            p.emit("scalar", lambda e: e.activation(out=brepC[:, :], in_=ps_e.ap()[:, T2:2 * T2],
                                                    func=AF.Copy), inc=sS)
            mbc = sS.n
            for dt in range(8):
                it = len(scan_marks)
                bi = it % 2
                colidx = dt * 16 + n
                wback = [(sV, scan_marks[-2][0])] if len(scan_marks) >= 2 else []
                whc = [scan_marks[-2][1]] if len(scan_marks) >= 2 else []
                p.emit("scalar", lambda e, dt=dt, bi=bi, colidx=colidx: e.activation(
                    out=dA_b[:, bi, :], in_=delta[:, dt, :], func=AF.Exp,
                    scale=a_sb.ap()[:, colidx:colidx + 1]),
                    waits=wback + ([(sV, mAv), (sV, mdu)] if (n == 0 and dt == 0) else []),
                    inc=sS)
                mda = sS.n
                gw0 = wback + ([(sS, mbc)] if dt == 0 else [])
                if n == 0 and dt == 0:
                    gw0 = gw0 + [(sV, mdu)]
                p.emit("gpsimd", lambda e, dt=dt, bi=bi: e.tensor_tensor(
                    dBu_b[:, bi, :], du_all[:, dt, :], brepB[:, :], OP.mult),
                    waits=gw0, inc=sG)
                mbu = sG.n
                p.emit("vector", lambda e, bi=bi, colidx=colidx: e.tensor_tensor_scan(
                    h_b[:, bi, :], dA_b[:, bi, :], dBu_b[:, bi, :],
                    states.ap()[:, colidx:colidx + 1], OP.mult, OP.add),
                    waits=[(sS, mda), (sG, mbu)] + whc, inc=sV)
                msc = sV.n
                p.emit("scalar", lambda e, bi=bi, colidx=colidx: e.activation(
                    out=states.ap()[:, colidx:colidx + 1], in_=h_b[:, bi, T2 - 1:T2],
                    func=AF.Copy), waits=[(sV, msc)], inc=sS)
                wyb = [(sT, scan_marks[-2][2])] if len(scan_marks) >= 2 else []
                if it % 2 == 0:
                    p.emit("vector", lambda e, bi=bi: e.tensor_mul(
                        m2_hC[:, bi, :], h_b[:, bi, :], brepC[:, :]),
                        waits=wyb + [(sV, msc)], inc=sV)
                    mhc = (sV, sV.n)
                else:
                    p.emit("gpsimd", lambda e, bi=bi: e.tensor_mul(
                        m2_hC[:, bi, :], h_b[:, bi, :], brepC[:, :]),
                        waits=wyb + [(sV, msc)], inc=sG)
                    mhc = (sG, sG.n)
                ps_y = ps2a if dt < 4 else ps2b
                off = (dt % 4) * T2
                p.emit("tensor", lambda e, ps_y=ps_y, off=off, bi=bi, n=n, dt=dt: e.matmul(
                    ps_y.ap()[:, off:off + T2], ident_bf.ap()[:, :], m2_hC[:, bi, :],
                    start=(n == 0 and dt % 2 == 0),
                    stop=(n == NST - 1 and dt % 2 == 1)),
                    waits=[mhc], inc=sT)
                scan_marks.append((msc, mhc, sT.n))
                if sc == 0 and q == 0 and it == 0:
                    tap("dA00", dA_b[:, 0, :], [(sS, mda)])
                    tap("dBu00", dBu_b[:, 0, :], [(sG, mbu)])
                    tap("h00", h_b[:, 0, :], [(sV, msc)])
                    tap("hC00", m2_hC[:, 0, :], [mhc])
        myd = sT.n
        for dt in range(8):
            ps_y = ps2a if dt < 4 else ps2b
            off = (dt % 4) * T2
            p.emit("vector", lambda e, dt=dt, ps_y=ps_y, off=off, cols=cols:
                   e.scalar_tensor_tensor(
                       t1_sb[:, :], u_bf[:, dt, cols], dvec_sb.ap()[:, dt:dt + 1],
                       ps_y.ap()[:, off:off + T2], OP.mult, OP.add),
                   waits=([(sT, myd), (sDL, wm["dvec"]), (sY, sY.n)] if dt == 0
                          else [(sV, sV.n)]), inc=sV)
            p.emit("vector", lambda e, dt=dt, cols=cols: e.tensor_mul(
                m2_y2[:, dt, :], t1_sb[:, :], sz_bf[:, dt, cols]),
                waits=[(sV, sV.n)], inc=sV)
        my2 = sV.n
        if sc == 0 and q == 0:
            tap("y20", m2_y2.rearrange("p a b -> p (a b)"), [(sV, my2)])
        p.dma("scalar", sY,
              y_in[sc].ap().rearrange("(dt pp) t -> pp dt t", pp=128)[:, :, cols],
              m2_y2[:, :, :],
              waits=([(sV, my2), (sC, sC.n)] if q == 0 else [(sV, my2)]))

    def m3_chunk(sc, mop):
        yo = y_out[sc]
        ap_in = bass.AP(yo, regs4[0], [[SC, 128], [SC * 128, 16], [1, TC]])
        p.dma("sync", sDL, yfull[:, :, :], ap_in, waits=[(sC, sC.n), (sT, sT.n)])
        mld = sDL.n
        rcols = slice(sc * TC, (sc + 1) * TC)
        for mc in range(8):
            psd = ps_c if mc % 2 == 0 else ps_d
            for kc in range(16):
                p.emit("tensor", lambda e, mc=mc, kc=kc, psd=psd: e.matmul(
                    psd.ap()[:, :], opw[:, kc, mc, :], yfull[:, kc, :],
                    start=(kc == 0), stop=(kc == 15)),
                    waits=([(sDL, mld), (sDW, mop), (sV, sV.n), (sS, sS.n)] if kc == 0 else []),
                    inc=(sT if kc == 15 else None))
            mt = sT.n
            p.emit("vector", lambda e, mc=mc, psd=psd, rcols=rcols: e.tensor_add(
                ln_t[:, :], r_sb.ap()[:, mc, rcols], psd.ap()[:, :]),
                waits=[(sT, mt)], inc=sV)
            mv = sV.n
            p.emit("scalar", lambda e, mc=mc, rcols=rcols: e.activation(
                out=r_sb.ap()[:, mc, rcols], in_=ln_t[:, :], func=AF.Copy),
                waits=[(sV, mv)], inc=sS)

    for sc in range(NSC if stage >= 3 else (1 if stage >= 2 else 0)):
        for q in range(SC // TC):
            m1_chunk(sc * (SC // TC) + q)
        if stage < 3:
            break
        mxp = sV.n
        p.dma("scalar", sAR, xp_in[sc].ap()[:, :], xp_part[0:96, :],
              waits=[(sV, mxp), (sC, sC.n)])
        mds = sAR.n
        p.emit("gpsimd", lambda e, sc=sc: e.collective_compute(
            "AllReduce", OP.add, replica_groups=GROUPS,
            ins=[xp_in[sc].ap().opt()], outs=[xp_out[sc].ap().opt()]),
            waits=[(sAR, mds)], inc=sC)
        mcc = sC.n
        p.dma("gpsimd", sDW, xp_dtb[0:64, :], xp_out[sc].ap()[0:64, :],
              waits=[(sC, mcc), (sT, sT.n)])
        mxpl = sDW.n
        p.emit("vector", None, waits=[(sDW, mxpl)], inc=sV)
        if sc == 0:
            tap("u0", u_bf.rearrange("p a b -> p (a b)"), [(sV, sV.n), (sS, sS.n)])
            tap("sz0", sz_bf.rearrange("p a b -> p (a b)"), [(sV, sV.n), (sS, sS.n)])
            tap("xpd0", xp_dtb[0:64, :], [(sDW, mxpl)])
            
        if stage < 4:
            continue
        for q in range(SC // T2):
            m2_chunk(sc, q, mxpl)
        mys = sY.n
        p.emit("gpsimd", lambda e, sc=sc: e.collective_compute(
            "AllGather", OP.bypass, replica_groups=GROUPS,
            ins=[y_in[sc].ap().opt()], outs=[y_out[sc].ap().opt()]),
            waits=[(sY, mys)], inc=sC)
    if stage >= 5:
        p.dma("gpsimd", sDW, opw[:, :, :, :],
              opw_d.rearrange("(kc k) (mc m) -> k kc mc m", k=128, m=128),
              waits=[(sT, sT.n)])
        mop = sDW.n
        for sc in range(NSC):
            m3_chunk(sc, mop)
        tap("r_mamba", r_sb.ap().rearrange("p a b -> p (a b)"), [(sS, sS.n)])

    def ln_chunk(tcl, dst, lt, lt2):
        rc = slice(tcl * TC, (tcl + 1) * TC)
        for mc in range(8):
            p.emit("vector", lambda e, mc=mc, rc=rc: e.tensor_copy(
                f_lbf[:, mc, :], r_sb.ap()[:, mc, rc]),
                waits=([(sS, sS.n), (sT, sT.n), (sLN, sLN.n)] if mc == 0 else []),
                inc=sV)
            p.emit("scalar", lambda e, mc=mc, rc=rc: e.activation(
                out=f_lsq[:, mc, :], in_=r_sb.ap()[:, mc, rc], func=AF.Square),
                waits=([(sV, sV.n)] if mc == 0 else []), inc=sS)
        mkv, mks = sV.n, sS.n
        for mc in range(8):
            p.emit("tensor", lambda e, mc=mc: e.matmul(
                ps_e.ap()[0:1, :], ones128.ap()[:, :], f_lbf[:, mc, :],
                start=(mc == 0), stop=(mc == 7)),
                waits=([(sV, mkv)] if mc == 0 else []), inc=(sT if mc == 7 else None))
        for mc in range(8):
            p.emit("tensor", lambda e, mc=mc: e.matmul(
                ps_c.ap()[0:1, :], ones128.ap()[:, :], f_lsq[:, mc, :],
                start=(mc == 0), stop=(mc == 7)),
                waits=([(sS, mks)] if mc == 0 else []), inc=(sT if mc == 7 else None))
        mkt = sT.n
        p.emit("vector", lambda e: e.tensor_scalar_mul(stf[0:1, 0:512],
                                                       ps_e.ap()[0:1, :], 1.0 / D),
               waits=[(sT, mkt)], inc=sV)
        p.emit("vector", lambda e: e.tensor_scalar_mul(stf[0:1, 512:1024],
                                                       ps_c.ap()[0:1, :], 1.0 / D), inc=sV)
        p.emit("vector", lambda e: e.tensor_mul(stf[0:1, 1024:1536],
                                                stf[0:1, 0:512],
                                                stf[0:1, 0:512]),
               waits=[(sV, sV.n)], inc=sV)
        p.emit("vector", lambda e: e.tensor_sub(stf[0:1, 512:1024],
                                                stf[0:1, 512:1024],
                                                stf[0:1, 1024:1536]),
               waits=[(sV, sV.n)], inc=sV)
        mv2 = sV.n
        p.emit("scalar", lambda e: e.activation(out=stf[0:1, 1024:1536],
                                                in_=stf[0:1, 512:1024],
                                                func=AF.Ln, bias=EPS),
               waits=[(sV, mv2)], inc=sS)
        p.emit("scalar", lambda e: e.activation(out=stf[0:1, 512:1024],
                                                in_=stf[0:1, 1024:1536],
                                                func=AF.Exp, scale=-0.5),
               waits=[(sS, sS.n)], inc=sS)
        ms = sS.n
        p.emit("vector", lambda e: e.scalar_tensor_tensor(
            stf[0:1, 0:512], stf[0:1, 0:512], -1.0, stf[0:1, 512:1024],
            OP.mult, OP.mult), waits=[(sS, ms)], inc=sV)
        p.emit("vector", lambda e: e.tensor_copy(stb[0:1, :], stf[0:1, 0:1024]),
               waits=[(sV, sV.n)], inc=sV)
        mv4 = sV.n
        p.emit("tensor", lambda e: e.matmul(ps_d.ap()[:, :], ones_col.ap()[0:1, :],
                                            stb[0:1, 512:1024], start=True, stop=True),
               waits=[(sV, mv4)], inc=sT)
        p.emit("tensor", lambda e: e.matmul(ps_e.ap()[:, :], ones_col.ap()[0:1, :],
                                            stb[0:1, 0:512], start=True, stop=True),
               inc=sT)
        mkt2 = sT.n
        for mc in range(8):
            p.emit("vector", lambda e, mc=mc, rc=rc, lt2=lt2: e.tensor_mul(
                lt2[:, :], r_sb.ap()[:, mc, rc], ps_d.ap()[:, :]),
                waits=([(sT, mkt2)] if mc == 0 else [(sV, sV.n)]), inc=sV)
            p.emit("vector", lambda e, mc=mc, dst=dst, lt2=lt2: e.tensor_add(
                dst[:, mc, :], lt2[:, :], ps_e.ap()[:, :]),
                waits=[(sV, sV.n)], inc=sV)
        return sV.n

    ffn_calls = [0]
    f_acc = vf(5376, 8, TC)

    def emit_ffn_final():
        emit_gating()
        ffn_calls[0] += 1
        first_call = ffn_calls[0] == 1
        if first_call:
            tap("gwf0", g_wfm[0:8, :, :].rearrange("p a b -> p (a b)"),
                [(sV, gate_mark[0])])
        w2free = [sT.n, sT.n]
        mevac = [0, 0]
        for tcl in range(ffn_lim[0]):
            bi = tcl % 2
            p.dma("sync", sDL, f_x2c[:, bi, :, :],
                  xn2_hbm.ap().rearrange("(kc k) t -> k kc t", k=128)[
                      :, :, tcl * TC:(tcl + 1) * TC],
                  waits=[(sT, sT.n), (sLN, sLN.n)])
            mld = sDL.n
            w1m = sT.n
            for e_i in range(ffn_lim[1]):
                p.dma("scalar", sW1, f_w1e[:, :, :, :],
                      w1b_d[e_i].rearrange("(kc k) (hc m) -> k kc hc m", k=128, m=128),
                      waits=([(sT, w1m), (sWC, mcast)]
                             if (tcl == 0 and e_i == 0) else [(sT, w1m)]))
                mw1 = sW1.n
                w2dmk = [0] * 8
                for mc in range(2):
                    p.dma("sync" if mc % 2 == 0 else "gpsimd", sW2[mc],
                          f_w2c[:, mc, :, :],
                          w2b_d[e_i, :, mc * 128:(mc + 1) * 128].rearrange(
                              "(hc k) m -> k hc m", k=128),
                          waits=([(sT, w2free[mc]), (sWC, mcast)]
                                 if (tcl == 0 and e_i == 0) else [(sT, w2free[mc])]))
                    w2dmk[mc] = sW2[mc].n
                p.emit("tensor", lambda e, e_i=e_i, tcl=tcl: e.matmul(
                    ps_e.ap()[:, :], esel.ap()[0:8, e_i, :], f_wfb[0:8, tcl, :],
                    start=True, stop=True),
                    waits=[(sV, sV.n), (sS, sS.n)], inc=sT)
                mre = sT.n
                p.emit("scalar", lambda e: e.activation(out=f_wr[:, :],
                                                        in_=ps_e.ap()[:, :], func=AF.Copy),
                       waits=[(sT, mre)], inc=sS)
                mwr = sS.n
                relmark = {}
                vmark = {}
                for hc in range(16):
                    psd = ps_c if hc % 2 == 0 else ps_d
                    if hc == 0:
                        w0 = [(sW1, mw1), (sDL, mld), (sS, mwr)]
                    elif hc == 1:
                        w0 = [(sS, mwr)]
                    else:
                        w0 = [(sS, relmark[hc - 2])]
                    for kc in range(8):
                        p.emit("tensor", lambda e, hc=hc, kc=kc, psd=psd, bi=bi: e.matmul(
                            psd.ap()[:, :], f_w1e[:, kc, hc, :], f_x2c[:, bi, kc, :],
                            start=(kc == 0), stop=(kc == 7)),
                            waits=(w0 if kc == 0 else []),
                            inc=(sT if kc == 7 else None))
                    mt = sT.n
                    p.emit("scalar", lambda e, psd=psd, e_i=e_i, hc=hc: e.activation(
                        out=f_rel[:, hc % 2, :], in_=psd.ap()[:, :], func=AF.Relu,
                        bias=b1_sb.ap()[:, e_i, hc:hc + 1]),
                        waits=[(sT, mt)]
                              + ([(sV, vmark[hc - 2])] if hc >= 2 else [(sV, sV.n)])
                              + ([(sDL, wm["b1"])] if hc == 0 else []),
                        inc=sS)
                    relmark[hc] = sS.n
                    p.emit("vector", lambda e, hc=hc: e.tensor_mul(
                        f_hp[:, hc, :], f_rel[:, hc % 2, :], f_wr[:, :]),
                        waits=[(sS, relmark[hc])], inc=sV)
                    vmark[hc] = sV.n
                mhp = sV.n
                if first_call and tcl == 0 and e_i == 0:
                    tap("fwr0", f_wr[:, :], [(sS, mwr)])
                    tap("fhp0", f_hp.rearrange("p a b -> p (a b)"), [(sV, mhp)])
                w1m = sT.n
                for mc in range(8):
                    bank = ps_e if mc % 2 == 0 else ps_h
                    for hc in range(16):
                        def mmfn(e, mc=mc, hc=hc, e_i=e_i, tcl=tcl, bank=bank):
                            if hc == 0 and e_i == 0:
                                e.matmul(bank.ap()[:, :], b2_sb.ap()[0:8, mc, :],
                                         f_wfb[0:8, tcl, :], start=True, stop=False)
                                return e.matmul(
                                    bank.ap()[:, :], f_w2c[:, mc % 2, hc, :],
                                    f_hp[:, hc, :], start=False, stop=False)
                            return e.matmul(
                                bank.ap()[:, :], f_w2c[:, mc % 2, hc, :],
                                f_hp[:, hc, :], start=(hc == 0), stop=(hc == 15))
                        p.emit("tensor", mmfn,
                               waits=([(sV, mhp), (sW2[mc % 2], w2dmk[mc]),
                                       (sV, mevac[mc % 2]), (sS, mwr)]
                                      if hc == 0 else []),
                               inc=(sT if hc == 15 else None))
                    w2free[mc % 2] = sT.n
                    mt2 = sT.n
                    if mc + 2 < 8:
                        nmc = mc + 2
                        p.dma("sync" if nmc % 2 == 0 else "gpsimd",
                              sW2[nmc % 2], f_w2c[:, nmc % 2, :, :],
                              w2b_d[e_i, :, nmc * 128:(nmc + 1) * 128].rearrange(
                                  "(hc k) m -> k hc m", k=128),
                              waits=[(sT, w2free[nmc % 2])])
                        w2dmk[nmc] = sW2[nmc % 2].n
                    if e_i == 0:
                        p.emit("vector", lambda e, mc=mc, bank=bank: e.tensor_copy(
                            f_acc[:, mc, :], bank.ap()[:, :]),
                            waits=[(sT, mt2)], inc=sV)
                    else:
                        p.emit("vector", lambda e, mc=mc, bank=bank: e.tensor_add(
                            f_acc[:, mc, :], f_acc[:, mc, :], bank.ap()[:, :]),
                            waits=[(sT, mt2)], inc=sV)
                    mevac[mc % 2] = sV.n
            if first_call and tcl == 0:
                tap("facc0", f_acc.rearrange("p a b -> p (a b)"), [(sV, sV.n)])
            rc = slice(tcl * TC, (tcl + 1) * TC)
            for mc in range(8):
                p.emit("vector", lambda e, mc=mc, rc=rc: e.tensor_add(
                    ln_t2F[:, :], r_sb.ap()[:, mc, rc], f_acc[:, mc, :]),
                    waits=([(sS, sS.n), (sV, sV.n)] if mc == 0 else [(sS, sS.n)]),
                    inc=sV)
                mvr = sV.n
                p.emit("scalar", lambda e, mc=mc, rc=rc: e.activation(
                    out=r_sb.ap()[:, mc, rc], in_=ln_t2F[:, :], func=AF.Copy),
                    waits=[(sV, mvr)], inc=sS)


    gate_mark = [0]

    def emit_gating():
        for tcl in range(4):
            bi = tcl % 2
            p.dma("sync", sDL, f_x2c[:, bi, :, :],
                  xn2_hbm.ap().rearrange("(kc k) t -> k kc t", k=128)[
                      :, :, tcl * TC:(tcl + 1) * TC],
                  waits=[(sT, sT.n), (sLN, sLN.n)])
            mld = sDL.n
            for kc in range(8):
                p.emit("tensor", lambda e, kc=kc, bi=bi: e.matmul(
                    ps_e.ap()[0:8, :], gw_sb.ap()[:, kc, :], f_x2c[:, bi, kc, :],
                    start=(kc == 0), stop=(kc == 7)),
                    waits=([(sDL, mld), (sDW, wm["gw"]), (sV, sV.n), (sS, sS.n)]
                           if kc == 0 else []),
                    inc=(sT if kc == 7 else None))
            mt = sT.n
            p.emit("vector", lambda e: e.tensor_scalar(
                ln_tF[0:8, :], ps_e.ap()[0:8, :], gb_sb.ap()[0:8, 0:1], None,
                OP.add, OP.bypass), waits=[(sT, mt), (sDL, wm["gb"])], inc=sV)
            mv = sV.n
            for j in range(4):
                p.emit("tensor", lambda e, j=j: e.transpose(
                    ps_c.ap()[:, j * 8:(j + 1) * 8],
                    ln_tF[0:8, j * 128:(j + 1) * 128], ident_f.ap()[0:8, 0:8]),
                    waits=([(sV, mv)] if j == 0 else []), inc=(sT if j == 3 else None))
            mtr = sT.n
            p.emit("vector", lambda e: e.tensor_copy(
                g_lgT[:, :, :], ps_c.ap()[:, 0:32].rearrange("p (a b) -> p a b", b=8)),
                waits=[(sT, mtr)], inc=sV)
            mg = sV.n
            p.emit("scalar", lambda e: e.activation(
                out=g_ex.rearrange("p a b -> p (a b)"),
                in_=g_lgT.rearrange("p a b -> p (a b)"), func=AF.Exp),
                waits=[(sV, mg)], inc=sS)
            msx = sS.n
            p.emit("vector", lambda e: e.reduce_sum(g_s[:, :], g_ex[:, :, :], axis=AX.X),
                   waits=[(sS, msx)], inc=sV)
            p.emit("vector", lambda e: e.reduce_max(g_m1[:, :], g_lgT[:, :, :], axis=AX.X),
                   inc=sV)
            mgm1 = sV.n
            for j in range(NE):
                p.emit("vector", lambda e, j=j: e.tensor_tensor(
                    g_wk[:, :, j], g_lgT[:, :, j], g_m1[:, :], OP.is_ge),
                    waits=([(sV, mgm1)] if j == 0 else []), inc=sV)
            for j in range(NE):
                p.emit("vector", lambda e, j=j: e.scalar_tensor_tensor(
                    g_l2[:, :, j], g_wk[:, :, j], NEG, g_lgT[:, :, j], OP.mult, OP.add),
                    waits=([(sV, sV.n)] if j == 0 else []), inc=sV)
            ml2 = sV.n
            p.emit("vector", lambda e: e.reduce_max(g_m2[:, :], g_l2[:, :, :], axis=AX.X),
                   waits=[(sV, ml2)], inc=sV)
            mv5 = sV.n
            p.emit("scalar", lambda e: e.activation(out=g_rc[:, :], in_=g_s[:, :],
                                                    func=AF.Ln), waits=[(sV, mv5)], inc=sS)
            p.emit("scalar", lambda e: e.activation(out=g_rc[:, :], in_=g_rc[:, :],
                                                    func=AF.Exp, scale=-1.0),
                   waits=[(sS, sS.n)], inc=sS)
            msr = sS.n
            for j in range(NE):
                p.emit("vector", lambda e, j=j: e.tensor_tensor(
                    g_wk[:, :, j], g_lgT[:, :, j], g_m2[:, :], OP.is_ge),
                    waits=([(sS, msr)] if j == 0 else []), inc=sV)
            for j in range(NE):
                p.emit("vector", lambda e, j=j: e.tensor_mul(
                    g_wk[:, :, j], g_wk[:, :, j], g_ex[:, :, j]),
                    waits=([(sV, sV.n)] if j == 0 else []), inc=sV)
            for j in range(NE):
                p.emit("vector", lambda e, j=j: e.scalar_tensor_tensor(
                    g_wk[:, :, j], g_wk[:, :, j], 1.0, g_rc[:, :], OP.mult, OP.mult),
                    waits=([(sV, sV.n)] if j == 0 else []), inc=sV)
            mw = sV.n
            for j in range(4):
                p.emit("tensor", lambda e, j=j: e.transpose(
                    ps_c.ap()[0:8, j * 128:(j + 1) * 128], g_wk[:, j, :],
                    ident_f.ap()[:, :]),
                    waits=([(sV, mw)] if j == 0 else []), inc=(sT if j == 3 else None))
            mtw = sT.n
            p.emit("vector", lambda e, tcl=tcl: e.tensor_copy(
                g_wfm[0:8, tcl, :], ps_c.ap()[0:8, :]), waits=[(sT, mtw)], inc=sV)
            p.emit("vector", lambda e, tcl=tcl: e.tensor_copy(
                f_wfb[0:8, tcl, :], g_wfm[0:8, tcl, :]),
                waits=[(sV, sV.n)], inc=sV)
        gate_mark[0] = sV.n

    if stage >= 6:
        for tcl in range(4):
            mdone = ln_chunk(tcl, xn3st, ln_tF, ln_t2F)
            if tcl == 0:
                tap("xn20", xn3st.rearrange("p a b -> p (a b)"), [(sV, mdone)])
            p.dma("scalar", sLN,
                  xn2_hbm.ap().rearrange("(kc k) t -> k kc t", k=128)[
                      :, :, tcl * TC:(tcl + 1) * TC],
                  xn3st[:, :, :], waits=[(sV, mdone)])
        emit_ffn_final()
        tap("r_ffn1", r_sb.ap().rearrange("p a b -> p (a b)"), [(sS, sS.n)])
    if stage >= 7:
        for tcl in range(4):
            mdone = ln_chunk(tcl, xn3st, ln_tF, ln_t2F)
            p.dma("scalar", sLN,
                  x3_in.ap().rearrange("(kc k) t -> k kc t", k=128)[
                      :, :, tcl * TC:(tcl + 1) * TC],
                  xn3st[:, :, :], waits=[(sV, mdone)])
        m3s = sLN.n
        p.emit("gpsimd", lambda e: e.collective_compute(
            "AllGather", OP.bypass, replica_groups=GROUPS,
            ins=[x3_in.ap().opt()], outs=[x3_out.ap().opt()]),
            waits=[(sLN, m3s)], inc=sC)
        m3c = sC.n

    if stage >= 8:
        for nm, (dst, src) in {
            "bo": (bo_sb[0:1, :], bo_d[:, :]),
            "bqk": (bqk_sb[0:1, :, :], bqk_d[:, :, :]),
            "bv": (bv_sb[0:1, :], bv_d[:, :]),
        }.items():
            p.dma("gpsimd", sDW, dst, src,
                  waits=[(sT, sT.n), (sV, sV.n), (sS, sS.n)])
        p.dma("gpsimd", sDW, at_wv[:, :, :],
              wv_d.rearrange("(kc k) m -> k kc m", k=128), waits=[(sT, sT.n)])
        mwv = sDW.n
        for nm in ("bo", "bqk", "bv"):
            wm[nm] = mwv

        def x3chunk_ap(j):
            rowbase = (j % 2) * D
            colbase = (j // 2) * TC
            return bass.AP(x3_out, rowbase * TO + colbase,
                           [[TO, 128], [TO * 128, 8], [1, TC]])

        for j in range(8):
            bi = j % 2
            p.dma("sync", sDL, at_x3[:, :, :], x3chunk_ap(j),
                  waits=[(sC, m3c), (sT, sT.n)])
            mld = sDL.n
            for tt in range(4):
                kt = j * 4 + tt
                for kc in range(8):
                    p.emit("tensor", lambda e, kc=kc, bi=bi, tt=tt: e.matmul(
                        ps_c.ap()[:, :], at_x3[:, kc, tt * 128:(tt + 1) * 128],
                        at_wv[:, kc, :], start=(kc == 0), stop=False),
                        waits=([(sDL, mld), (sDW, mwv), (sS, sS.n)] if (tt == 0 and kc == 0)
                               else ([(sS, sS.n)] if kc == 0 else [])),
                        inc=(sT if kc == 7 else None))
                p.emit("tensor", lambda e: e.matmul(
                    ps_c.ap()[:, :], ones_col.ap()[0:1, :], bv_sb[0:1, :],
                    start=False, stop=True), waits=[(sDW, wm["bv"])], inc=sT)
                mt = sT.n
                p.emit("scalar", lambda e, kt=kt: e.activation(
                    out=at_vtok[:, kt, :, 0:64], in_=ps_c.ap()[:, :], func=AF.Copy),
                    waits=[(sT, mt)], inc=sS)
        p.emit("vector", lambda e: e.memset(at_vtok[:, :, :, 64:72], 1.0),
               waits=[(sLN, sLN.n)], inc=sV)
        mvt = sS.n
        mvts = sV.n

        for hp in range(4):
            for qk in range(2):
                p.dma("gpsimd", sDW, at_wqk[:, :, qk, :],
                      bass.AP(wqk_d.tensor, qk * NH * DH + hp * 128,
                              [[2 * NH * DH, 128], [2 * NH * DH * 128, 8], [1, 128]]),
                      waits=[(sT, sT.n)] if qk == 0 else [])
            mwqk = sDW.n
            mva = mvts
            for j in range(8):
                bi = j % 2
                p.dma("sync", sDL, at_x3[:, :, :], x3chunk_ap(j),
                      waits=[(sT, sT.n)])
                mld = sDL.n
                for qk in range(2):
                    psd = ps_c if qk == 0 else ps_d
                    dst = at_q if qk == 0 else at_k
                    for kc in range(8):
                        p.emit("tensor", lambda e, kc=kc, qk=qk, bi=bi, psd=psd: e.matmul(
                            psd.ap()[:, :], at_wqk[:, kc, qk, :], at_x3[:, kc, :],
                            start=(kc == 0), stop=False),
                            waits=([(sDL, mld), (sDW, mwqk), (sV, sV.n), (sS, sS.n)]
                                   if kc == 0 else []),
                            inc=(sT if kc == 7 else None))
                    p.emit("tensor", lambda e, qk=qk, psd=psd, hp=hp: e.matmul(
                        psd.ap()[:, :],
                        bqk_sb.rearrange("p a b -> p (a b)")[0:1, qk * NH * DH + hp * 128:
                                                             qk * NH * DH + hp * 128 + 128],
                        ones_row.ap()[0:1, :], start=False, stop=True),
                        waits=[(sDW, wm["bqk"])], inc=sT)
                    mt = sT.n
                    p.emit("vector", lambda e, j=j, psd=psd, dst=dst: e.tensor_copy(
                        dst[:, j, :], psd.ap()[:, :]), waits=[(sT, mt)], inc=sV)
            mqk = sV.n
            for hh in range(2):
                hl = 2 * hp + hh
                rb = hh * 64
                for qT in range(8):
                    nkt = 4 * (qT + 1)
                    av_ps = ps_c if (qT % 2 == 0) else ps_d
                    npairs = (nkt + 1) // 2
                    for pr in range(npairs):
                        sb = ps2a if (pr % 2 == 0) else ps2b
                        pbuf = pr % 2
                        for sl in range(2):
                            kt = pr * 2 + sl
                            if kt >= nkt:
                                continue
                            dg = kt - 4 * qT
                            p.emit("tensor", lambda e, sb=sb, sl=sl, rb=rb, kt=kt,
                                   qT=qT, dg=dg: e.matmul(
                                       sb.ap()[:, sl * TC:(sl + 1) * TC],
                                       at_k[rb:rb + 64, kt // 4,
                                            (kt % 4) * 128:(kt % 4) * 128 + 128],
                                       at_q[rb:rb + 64, qT, :], start=True, stop=(dg < 0)),
                                   waits=([(sV, mqk), (sS, sS.n)] if sl == 0 else []),
                                   inc=(None if dg >= 0 else
                                        (sT if (sl == 1 or kt == nkt - 1) else None)))
                            if dg >= 0:
                                for z in range(dg):
                                    p.emit("tensor", lambda e, sb=sb, sl=sl, z=z: e.matmul(
                                        sb.ap()[:, sl * TC + z * 128:sl * TC + (z + 1) * 128],
                                        ident_bf.ap()[:, :], negt_bf.ap()[:, :],
                                        start=False, stop=False), inc=None)
                                p.emit("tensor", lambda e, sb=sb, sl=sl, dg=dg: e.matmul(
                                    sb.ap()[:, sl * TC + dg * 128:sl * TC + (dg + 1) * 128],
                                    ident_bf.ap()[:, :], trimask_bf.ap()[:, :],
                                    start=False, stop=True),
                                    inc=(sT if (sl == 1 or kt == nkt - 1) else None))
                        mts = sT.n
                        wid = min(2, nkt - pr * 2)
                        p.emit("scalar", lambda e, sb=sb, pbuf=pbuf, wid=wid: e.activation(
                            out=at_p[:, pbuf, 0:wid, :].rearrange("p a b -> p (a b)"),
                            in_=sb.ap()[:, 0:wid * TC], func=AF.Exp, scale=0.125),
                            waits=[(sT, mts)], inc=sS)
                        msp = sS.n
                        for sl in range(wid):
                            kt = pr * 2 + sl
                            p.emit("tensor", lambda e, av_ps=av_ps, kt=kt, hh=hh, pbuf=pbuf,
                                   sl=sl, nkt=nkt, hl=hl: e.matmul(
                                       av_ps.ap()[0:72, :], at_vtok[:, kt, hl, :],
                                       at_p[:, pbuf, sl, :], start=(kt == 0),
                                       stop=(kt == nkt - 1)),
                                   waits=([(sS, msp), (sV, sV.n)] if kt == 0
                                          else ([(sS, msp)] if sl == 0 else [])),
                                   inc=(sT if kt == nkt - 1 else None))
                    mav = sT.n
                    p.emit("scalar", lambda e, qT=qT, av_ps=av_ps: e.activation(
                        out=at_den[64:72, qT, :], in_=av_ps.ap()[64:72, :], func=AF.Copy),
                        waits=[(sT, mav)], inc=sS)
                    p.emit("vector", lambda e, hh=hh, qT=qT, av_ps=av_ps: e.tensor_copy(
                        at_sg[0:64, hh, qT, :], av_ps.ap()[0:64, :]),
                        waits=[(sT, mav)], inc=sV)
                msg = sV.n
                mden = sS.n
                p.emit("scalar", lambda e: e.activation(
                    out=at_den[64:72, :, :].rearrange("p a b -> p (a b)"),
                    in_=at_den[64:72, :, :].rearrange("p a b -> p (a b)"),
                    func=AF.Ln), waits=[(sS, mden)], inc=sS)
                p.emit("scalar", lambda e: e.activation(
                    out=at_denb[64:72, :, :].rearrange("p a b -> p (a b)"),
                    in_=at_den[64:72, :, :].rearrange("p a b -> p (a b)"),
                    func=AF.Exp, scale=-1.0), waits=[(sS, sS.n)], inc=sS)
                mrc = sS.n
                for qT in range(8):
                    p.emit("tensor", lambda e, qT=qT: e.matmul(
                        ps_e.ap()[0:64, :], rowsel.ap()[64:72, :],
                        at_denb[64:72, qT, :], start=True, stop=True),
                        waits=[(sS, mrc), (sV, sV.n), (sAI, sAI.n)], inc=sT)
                    mre = sT.n
                    p.emit("vector", lambda e, hh=hh, qT=qT: e.tensor_mul(
                        at_onr[0:64, qT % 2, :], at_sg[0:64, hh, qT, :], ps_e.ap()[0:64, :]),
                        waits=[(sT, mre), (sV, msg)], inc=sV)
                    if qT % 2 == 1:
                        mno = sV.n
                        p.dma("scalar", sAI,
                              bass.AP(at_in, hl * 64 * L + (qT - 1) * TC,
                                      [[L, 64], [1, 2 * TC]]),
                              at_onr[0:64, :, :], waits=[(sV, mno)])
        mats = sAI.n
        p.emit("gpsimd", lambda e: e.collective_compute(
            "AllGather", OP.bypass, replica_groups=GROUPS,
            ins=[at_in.ap().opt()], outs=[at_out.ap().opt()]),
            waits=[(sAI, mats)], inc=sC)
        mac = sC.n
        p.dma("gpsimd", sDW, at_wo[:, :, :, :], wo_d[:, :, :, :], waits=[(sT, sT.n)])
        mwo = sDW.n
        for j in range(4):
            ap_at = bass.AP(at_out, regs4[j], [[L, 128], [L * 128, 8], [1, TC]])
            p.dma("sync", sDL, at_at[:, :, :], ap_at, waits=[(sC, mac), (sT, sT.n)])
            mld = sDL.n
            rc = slice(j * TC, (j + 1) * TC)
            for mc in range(8):
                psd = ps_c if mc % 2 == 0 else ps_d
                for hp2 in range(8):
                    p.emit("tensor", lambda e, mc=mc, hp2=hp2, psd=psd: e.matmul(
                        psd.ap()[:, :], at_wo[:, hp2, mc, :], at_at[:, hp2, :],
                        start=(hp2 == 0), stop=False),
                        waits=([(sDL, mld), (sDW, mwo), (sV, sV.n), (sS, sS.n)]
                               if (mc == 0 and hp2 == 0)
                               else ([(sV, sV.n)] if hp2 == 0 else [])),
                        inc=None)
                p.emit("tensor", lambda e, mc=mc, psd=psd: e.matmul(
                    psd.ap()[:, :], bo_sb[0:1, mc * 128:(mc + 1) * 128],
                    ones_row.ap()[0:1, :], start=False, stop=True),
                    waits=[(sDW, wm["bo"])], inc=sT)
                mt = sT.n
                p.emit("vector", lambda e, mc=mc, psd=psd, rc=rc: e.tensor_add(
                    ln_tF[:, :], r_sb.ap()[:, mc, rc], psd.ap()[:, :]),
                    waits=[(sT, mt), (sS, sS.n)], inc=sV)
                mv = sV.n
                p.emit("scalar", lambda e, mc=mc, rc=rc: e.activation(
                    out=r_sb.ap()[:, mc, rc], in_=ln_tF[:, :], func=AF.Copy),
                    waits=[(sV, mv)], inc=sS)
        tap("r_attn", r_sb.ap().rearrange("p a b -> p (a b)"), [(sS, sS.n)])

    if stage >= 9:
        for tcl in range(4):
            mdone = ln_chunk(tcl, xn3st, ln_tF, ln_t2F)
            p.dma("scalar", sLN,
                  xn2_hbm.ap().rearrange("(kc k) t -> k kc t", k=128)[
                      :, :, tcl * TC:(tcl + 1) * TC],
                  xn3st[:, :, :], waits=[(sV, mdone)])
        emit_ffn_final()

    fin_t = vf(4352, D)
    for tt in range(16):
        for half in range(2):
            psd = ps_c if half == 0 else ps_d
            for jj in range(4):
                kc = half * 4 + jj
                p.emit("tensor", lambda e, kc=kc, jj=jj, psd=psd, tt=tt: e.transpose(
                    psd.ap()[:, jj * 128:(jj + 1) * 128],
                    r_sb.ap()[:, kc, tt * 128:(tt + 1) * 128], ident_f.ap()[:, :]),
                    waits=([(sS, sS.n), (sV, sV.n)] if jj == 0 else []),
                    inc=(sT if jj == 3 else None))
            mt = sT.n
            p.emit("vector", lambda e, half=half, psd=psd: e.tensor_copy(
                fin_t[:, half * 512:(half + 1) * 512], psd.ap()[:, :]),
                waits=[(sT, mt), (sO, sO.n)], inc=sV)
        mv = sV.n
        p.dma("scalar", sO, out_d[tt * 128:(tt + 1) * 128, :], fin_t[:, :],
              waits=[(sV, mv)])
    p.emit("sync", None, waits=[(sO, sO.n), (sDW, sDW.n), (sDS, sDS.n),
                                (sY, sY.n), (sAR, sAR.n), (sLN, sLN.n),
                                (sAI, sAI.n), (sW1, sW1.n), (sWC, sWC.n),
                                (sW2[0], sW2[0].n), (sW2[1], sW2[1].n)])
    p.finalize()
    return


def _host_inputs(inputs, core):
    g = core % 2
    b = core // 2
    f32 = np.float32
    hs = np.asarray(inputs["L0_attn_a_hidden_states"][b], dtype=f32)
    own_rows = np.concatenate([np.arange((2 * j + g) * 512, (2 * j + g) * 512 + 512)
                               for j in range(4)])
    ch0 = g * DIH
    ipw = np.asarray(inputs["L0_attn_a_in_proj_w"], dtype=f32)
    d = {
        "hid": np.ascontiguousarray(hs),
        "hidown": np.ascontiguousarray(hs[own_rows]),
        "gofs": np.asarray([[g * 512 + j * 1024 for j in range(4)]], dtype=np.int32),
        "inxw": np.ascontiguousarray(ipw[ch0:ch0 + DIH, :].T),
        "inzw": np.ascontiguousarray(ipw[2048 + ch0:2048 + ch0 + DIH, :].T),
        "convw": np.ascontiguousarray(
            np.asarray(inputs["L0_attn_a_conv_w"], dtype=f32)[ch0:ch0 + DIH, 0, :]
            .reshape(8, 128, 4).transpose(1, 0, 2)),
        "convb": np.ascontiguousarray(
            np.asarray(inputs["L0_attn_a_conv_b"], dtype=f32)[ch0:ch0 + DIH]
            .reshape(8, 128).T),
        "xpw": np.ascontiguousarray(
            np.asarray(inputs["L0_attn_a_x_proj_w"], dtype=f32)[:, ch0:ch0 + DIH].T),
        "dtw": np.ascontiguousarray(
            np.asarray(inputs["L0_attn_a_dt_w"], dtype=f32)[ch0:ch0 + DIH, :].T),
        "dtb": np.ascontiguousarray(
            np.asarray(inputs["L0_attn_a_dt_b"], dtype=f32)[ch0:ch0 + DIH]
            .reshape(8, 128).T),
        "alog": np.ascontiguousarray(
            np.asarray(inputs["L0_attn_a_A_log"], dtype=f32)[ch0:ch0 + DIH]
            .reshape(8, 128, NST).transpose(1, 0, 2).reshape(128, 8 * NST)),
        "dvec": np.ascontiguousarray(
            np.asarray(inputs["L0_attn_a_D"], dtype=f32)[ch0:ch0 + DIH]
            .reshape(8, 128).T),
        "opw": np.ascontiguousarray(
            np.asarray(inputs["L0_attn_a_out_proj_w"], dtype=f32).T),
        "wo": np.ascontiguousarray(
            np.asarray(inputs["L0_attn_b_Wo"], dtype=f32)
            .reshape(8, 128, 8, 128).transpose(1, 0, 2, 3)),
        "bo": np.asarray(inputs["L0_attn_b_bo"], dtype=f32).reshape(1, D),
        "gw": np.ascontiguousarray(np.asarray(inputs["L0_ffn_gate_w"], dtype=f32)),
        "gb": np.asarray(inputs["L0_ffn_gate_b"], dtype=f32).reshape(NE, 1),
        "w1": np.ascontiguousarray(np.asarray(inputs["L0_ffn_w1"], dtype=f32)),
        "b1": np.ascontiguousarray(
            np.asarray(inputs["L0_ffn_b1"], dtype=f32).reshape(NE, 16, 128)
            .transpose(2, 0, 1)),
        "w2": np.ascontiguousarray(np.asarray(inputs["L0_ffn_w2"], dtype=f32)),
        "b2": np.ascontiguousarray(np.asarray(inputs["L0_ffn_b2"], dtype=f32)),
    }
    h0 = g * NH
    wq = np.asarray(inputs["L0_attn_b_Wq"], dtype=f32)[h0:h0 + NH]
    wk = np.asarray(inputs["L0_attn_b_Wk"], dtype=f32)[h0:h0 + NH]
    wv = np.asarray(inputs["L0_attn_b_Wv"], dtype=f32)[h0:h0 + NH]
    d["wqk"] = np.ascontiguousarray(np.stack(
        [wq.transpose(1, 0, 2).reshape(D, NH * DH),
         wk.transpose(1, 0, 2).reshape(D, NH * DH)], axis=1))
    d["wv"] = np.ascontiguousarray(wv.transpose(1, 0, 2).reshape(D, NH * DH))
    bq = np.asarray(inputs["L0_attn_b_bq"], dtype=f32)[h0:h0 + NH].reshape(NH * DH)
    bk = np.asarray(inputs["L0_attn_b_bk"], dtype=f32)[h0:h0 + NH].reshape(NH * DH)
    d["bqk"] = np.ascontiguousarray(np.stack([bq, bk])[None, :, :])
    d["bv"] = np.ascontiguousarray(
        np.asarray(inputs["L0_attn_b_bv"], dtype=f32)[h0:h0 + NH].reshape(1, NH * DH))
    return d


def _build_nc(dbg=None, stage=99, ffn_lim=(4, 8)):
    nc = bass.Bass("TRN2", target_bir_lowering=False)
    build(nc, dbg=dbg, stage=stage, ffn_lim=ffn_lim)
    return nc


def kernel(**inputs):
    nc = _build_nc()
    in_maps = [_host_inputs(inputs, core) for core in range(8)]
    res = run_bass_kernel_spmd(nc, in_maps, core_ids=list(range(8)))
    out = np.zeros((4, L, D), dtype=np.float32)
    for core in range(8):
        g = core % 2
        b = core // 2
        o = np.asarray(res.results[core]["out"])
        for j in range(4):
            gt = (2 * j + g) * 512
            out[b, gt:gt + 512, :] = o[j * 512:(j + 1) * 512, :]
    return out
